# version 01
# baseline (speedup 1.0000x reference)
import math
from contextlib import ExitStack

import numpy as np
import ml_dtypes
import concourse.bass as bass
import concourse.mybir as mybir
from concourse.bass_utils import run_bass_kernel_spmd

F32 = mybir.dt.float32
BF16 = mybir.dt.bfloat16
AF = mybir.ActivationFunctionType
ALU = mybir.AluOpType

B, S, E, H, DH, FF = 2, 4096, 4096, 32, 128, 11008
NEC = E // 128
NFC = FF // 128
NTT = S // 128
NOWN = 8
NEP = E // 512
SCALING = float(128 ** -0.5)
BSTAT = 30.0
MASK_NEG = -1.0e36


def _alibi_slopes():
    base = 2.0 ** (-(2.0 ** -(math.log2(32) - 3)))
    return (base ** np.arange(1, 33, dtype=np.float64)).astype(np.float32)


class _Op:
    __slots__ = ("q", "waits", "fn", "inc")

    def __init__(self, q, waits, fn, inc):
        self.q, self.waits, self.fn, self.inc = q, waits, fn, inc


class Builder:
    QUEUES = {"pe": 1, "dve": 1, "act": 1, "sync": 16, "gp": 16}
    ENG = {"pe": "tensor", "dve": "vector", "act": "scalar",
           "sync": "sync", "gp": "gpsimd"}

    def __init__(self, nc, stack):
        self.nc = nc
        self.sems = {q: stack.enter_context(nc.semaphore(f"ctr_{q}"))
                     for q in self.QUEUES}
        self.count = {q: 0 for q in self.QUEUES}
        self.hw = {q: {} for q in self.QUEUES}
        self.pending = []

    def op(self, q, fn, deps=(), inc=True):
        merged = {}
        for tok in deps:
            if tok is None:
                continue
            dq, dv = tok
            if dq == q and q == "pe":
                continue
            if merged.get(dq, 0) < dv:
                merged[dq] = dv
        hw = self.hw[q]
        waits = []
        for dq, dv in merged.items():
            if hw.get(dq, 0) >= dv:
                continue
            waits.append((self.sems[dq], dv))
            hw[dq] = dv
        qty = self.QUEUES[q] if inc else 0
        tok = None
        if inc:
            self.count[q] += qty
            tok = (q, self.count[q])
        self.pending.append(_Op(q, waits, fn, qty))
        return tok

    def gate(self):
        return [(q, n) for q, n in self.count.items() if n > 0]

    def raw(self, q, fn):
        self.pending.append(_Op(q, [], fn, 0))

    def flush(self):
        ops = self.pending
        self.pending = []
        if not ops:
            return
        with self.nc.Block() as blk:
            for q, eng_attr in self.ENG.items():
                qops = [o for o in ops if o.q == q]
                if not qops:
                    continue

                def emit(eng, qops=qops, q=q):
                    for o in qops:
                        for sem, val in o.waits:
                            eng.wait_ge(sem, val)
                        inst = o.fn(eng)
                        if o.inc:
                            inst.then_inc(self.sems[q], o.inc)

                getattr(blk, eng_attr)(emit)


class Ring:

    def __init__(self, n):
        self.n = n
        self.wr = [None] * n
        self.rd = [[] for _ in range(n)]

    def war(self, s):
        deps = list(self.rd[s])
        self.rd[s] = []
        if self.wr[s] is not None:
            deps.append(self.wr[s])
        return deps

    def wrote(self, s, tok):
        self.wr[s] = tok

    def read(self, s, tok):
        if tok is not None:
            self.rd[s].append(tok)


def build_nc(debug=False):
    nc = bass.Bass("TRN2", target_bir_lowering=False)

    xkv = nc.dram_tensor("xkv", [S, E], F32, kind="ExternalInput").ap()
    xown = nc.dram_tensor("xown", [NOWN, 128, E], F32, kind="ExternalInput").ap()
    wkv = nc.dram_tensor("wkv", [E, 256], F32, kind="ExternalInput").ap()
    wq = nc.dram_tensor("wq", [H, E, 128], F32, kind="ExternalInput").ap()
    wc = nc.dram_tensor("wc", [E, E], F32, kind="ExternalInput").ap()
    wg = nc.dram_tensor("wg", [NFC, 128, NEC, 128], F32, kind="ExternalInput").ap()
    wu = nc.dram_tensor("wu", [NFC, 128, NEC, 128], F32, kind="ExternalInput").ap()
    wd = nc.dram_tensor("wd", [NEP, NFC, 128, 512], F32, kind="ExternalInput").ap()
    dmc = nc.dram_tensor("dmc", [128, 1024], F32, kind="ExternalInput").ap()
    biasc = nc.dram_tensor("biasc", [128, H * 8], F32, kind="ExternalInput").ap()
    identc = nc.dram_tensor("identc", [128, 128], F32, kind="ExternalInput").ap()
    mscc = nc.dram_tensor("mscc", [128, 8], F32, kind="ExternalInput").ap()
    out = nc.dram_tensor("out", [NOWN, 128, E], F32, kind="ExternalOutput").ap()

    skind = "ExternalOutput" if debug else "Internal"
    attn_d = nc.dram_tensor("attn_d", [H, 128, 1024], BF16, kind=skind).ap()
    s_d = nc.dram_tensor("s_d", [1024, E], F32, kind=skind).ap()
    h_d = nc.dram_tensor("h_d", [NFC, 128, 1024], BF16, kind=skind).ap()
    if debug:
        dbg_xnt = nc.dram_tensor("dbg_xnt", [128, NEC * 1024], BF16, kind="ExternalOutput").ap()
        dbg_q = nc.dram_tensor("dbg_q", [128, H * 1024], BF16, kind="ExternalOutput").ap()
        dbg_k = nc.dram_tensor("dbg_k", [128, S], BF16, kind="ExternalOutput").ap()
        dbg_v = nc.dram_tensor("dbg_v", [128, NTT * 128], BF16, kind="ExternalOutput").ap()

    slopes = _alibi_slopes()

    with ExitStack() as top:
        bld = Builder(nc, top)
        op = bld.op

        e = top.enter_context
        cst_dm = e(nc.sbuf_tensor("cst_dm", [128, 1024], F32))
        bias_sb = e(nc.sbuf_tensor("bias_sb", [128, H * 8], F32))
        msc_sb = e(nc.sbuf_tensor("msc_sb", [128, 8], F32))
        id_bf = e(nc.sbuf_tensor("id_bf", [128, 128], BF16))
        k_fm = e(nc.sbuf_tensor("k_fm", [128, S], BF16))
        v_tm = e(nc.sbuf_tensor("v_tm", [128, NTT * 128], BF16))

        t_dm = op("sync", lambda s: s.dma_start(out=cst_dm[:], in_=dmc))
        t_bias = op("sync", lambda s: s.dma_start(out=bias_sb[:], in_=biasc))
        t_msc = op("sync", lambda s: s.dma_start(out=msc_sb[:], in_=mscc))
        t_id = op("gp", lambda g: g.dma_start(out=id_bf[:], in_=identc))

        kv_tok = [None] * NTT
        k_tok = [None] * NTT

        with ExitStack() as sc1:
            e1 = sc1.enter_context
            x_t = [e1(nc.sbuf_tensor(f"x_t{i}", [128, E], F32)) for i in range(2)]
            xn_t = [e1(nc.sbuf_tensor(f"xn_t{i}", [128, E], BF16)) for i in range(2)]
            xnt_t = [e1(nc.sbuf_tensor(f"xnt_t{i}", [128, NEC * 128], BF16)) for i in range(2)]
            wkv_sb = e1(nc.sbuf_tensor("wkv_sb", [128, NEC * 256], BF16))
            bn_t = [e1(nc.sbuf_tensor(f"bn_t{i}", [128, 48], F32)) for i in range(2)]
            mv_t = [e1(nc.sbuf_tensor(f"mv_t{i}", [128, 2], F32)) for i in range(2)]
            sd_t = [e1(nc.sbuf_tensor(f"sd_t{i}", [128, 1], F32)) for i in range(2)]
            rs_t = [e1(nc.sbuf_tensor(f"rs_t{i}", [128, 1], F32)) for i in range(2)]
            nm_t = [e1(nc.sbuf_tensor(f"nm_t{i}", [128, 1], F32)) for i in range(2)]
            ktm_t = [e1(nc.sbuf_tensor(f"ktm_t{i}", [128, 128], BF16)) for i in range(2)]
            ps_tr = [e1(nc.psum_tensor(f"ps_tr{i}", [128, 1024], BF16)) for i in range(2)]
            ps_kv = [e1(nc.psum_tensor(f"ps_kv{i}", [128, 256], F32)) for i in range(2)]
            ps_kt = [e1(nc.psum_tensor(f"ps_kt{i}", [128, 128], BF16)) for i in range(2)]

            wkv_r = wkv.rearrange("(ec p) c -> p ec c", p=128)
            wkv_dst = wkv_sb[:].rearrange("p (ec c) -> p ec c", ec=NEC)
            t_wkv = op("gp", lambda g: g.dma_start(out=wkv_dst, in_=wkv_r,
                                                   max_dma_last_dim=8192))

            rX, rXN, rXNT = Ring(2), Ring(2), Ring(2)
            rBN, rMV, rSD, rRS, rNM = Ring(2), Ring(2), Ring(2), Ring(2), Ring(2)
            rKTM, rPTR, rPKV, rPKT = Ring(2), Ring(2), Ring(2), Ring(2)

            def ln_pipeline(src_ap_fn, n_tiles, xnt_dest_fn, kv_consume):
                for tt in range(n_tiles):
                    sl = tt % 2
                    t_x = op("gp", lambda g, tt=tt, sl=sl: g.dma_start(
                        out=x_t[sl][:], in_=src_ap_fn(tt)), deps=rX.war(sl))
                    rX.wrote(sl, t_x)
                    stat_toks = []
                    for i in range(8):
                        t = op("dve", lambda v, sl=sl, i=i: v.bn_stats(
                            bn_t[sl][:, i * 6:(i + 1) * 6],
                            x_t[sl][:, i * 512:(i + 1) * 512]),
                            deps=[t_x] + (rBN.war(sl) if i == 0 else []))
                        stat_toks.append(t)
                        rX.read(sl, t)
                    rBN.wrote(sl, stat_toks[-1])
                    t_mv = op("dve", lambda v, sl=sl: v.bn_aggr(
                        mv_t[sl][:], bn_t[sl][:]), deps=stat_toks + rMV.war(sl))
                    rMV.wrote(sl, t_mv)
                    rBN.read(sl, t_mv)
                    t_sd = op("act", lambda s, sl=sl: s.activation(
                        sd_t[sl][:], mv_t[sl][:, 1:2], AF.Sqrt,
                        bias=msc_sb[:, 0:1]), deps=[t_mv, t_msc] + rSD.war(sl))
                    rSD.wrote(sl, t_sd)
                    t_rs = op("dve", lambda v, sl=sl: v.reciprocal(
                        rs_t[sl][:], sd_t[sl][:]), deps=[t_sd] + rRS.war(sl))
                    rRS.wrote(sl, t_rs)
                    rSD.read(sl, t_rs)
                    t_nm = op("dve", lambda v, sl=sl: v.tensor_scalar(
                        nm_t[sl][:], mv_t[sl][:, 0:1], rs_t[sl][:, 0:1], -1.0,
                        ALU.mult, ALU.mult), deps=[t_rs] + rNM.war(sl))
                    rNM.wrote(sl, t_nm)
                    rMV.read(sl, t_nm)
                    t_xn = op("act", lambda s, sl=sl: s.activation(
                        xn_t[sl][:], x_t[sl][:], AF.Identity,
                        bias=nm_t[sl][:, 0:1], scale=rs_t[sl][:, 0:1]),
                        deps=[t_nm, t_x] + rXN.war(sl))
                    rXN.wrote(sl, t_xn)
                    rX.read(sl, t_xn)
                    rRS.read(sl, t_xn)
                    rNM.read(sl, t_xn)
                    ev_toks = []
                    for g in range(4):
                        ps = g % 2
                        tr_tok = None
                        war = rPTR.war(ps)
                        for j in range(8):
                            ec = g * 8 + j
                            tr_tok = op("pe", lambda t, sl=sl, ps=ps, j=j, ec=ec:
                                        t.transpose(ps_tr[ps][:, j * 128:(j + 1) * 128],
                                                    xn_t[sl][:, ec * 128:(ec + 1) * 128],
                                                    id_bf[:]),
                                        deps=[t_xn, t_id] + (war if j == 0 else []),
                                        inc=(j == 7))
                        rPTR.wrote(ps, tr_tok)
                        rXN.read(sl, tr_tok)
                        dst, dslot = xnt_dest_fn(tt, g)
                        dep_war = rXNT.war(dslot) if dslot is not None else []
                        t_ev = op("dve", lambda v, dst=dst, ps=ps: v.tensor_copy(
                            dst, ps_tr[ps][:]), deps=[tr_tok] + dep_war)
                        if dslot is not None:
                            rXNT.wrote(dslot, t_ev)
                        rPTR.read(ps, t_ev)
                        ev_toks.append(t_ev)
                    if not kv_consume:
                        continue
                    mm_tok = None
                    war = rPKV.war(sl)
                    for ec in range(NEC):
                        mm_tok = op("pe", lambda t, sl=sl, ec=ec: t.matmul(
                            ps_kv[sl][:],
                            xnt_t[sl][:, ec * 128:(ec + 1) * 128],
                            wkv_sb[:, ec * 256:(ec + 1) * 256],
                            start=(ec == 0), stop=(ec == NEC - 1)),
                            deps=[ev_toks[ec // 8], t_wkv] + (war if ec == 0 else []),
                            inc=(ec == NEC - 1))
                    rPKV.wrote(sl, mm_tok)
                    for t_ev in ev_toks:
                        rXNT.read(sl, mm_tok)
                    t_ks = op("dve", lambda v, sl=sl: v.tensor_scalar(
                        ktm_t[sl][:], ps_kv[sl][:, 0:128], SCALING, None,
                        ALU.mult), deps=[mm_tok] + rKTM.war(sl))
                    rKTM.wrote(sl, t_ks)
                    rPKV.read(sl, t_ks)
                    t_vc = op("dve", lambda v, sl=sl, tt=tt: v.tensor_copy(
                        v_tm[:, tt * 128:(tt + 1) * 128], ps_kv[sl][:, 128:256]),
                        deps=[mm_tok])
                    rPKV.read(sl, t_vc)
                    kv_tok[tt] = t_vc
                    t_kt = op("pe", lambda t, sl=sl: t.transpose(
                        ps_kt[sl][:], ktm_t[sl][:], id_bf[:]),
                        deps=[t_ks, t_id] + rPKT.war(sl))
                    rPKT.wrote(sl, t_kt)
                    rKTM.read(sl, t_kt)
                    t_ke = op("dve", lambda v, sl=sl, tt=tt: v.tensor_copy(
                        k_fm[:, tt * 128:(tt + 1) * 128], ps_kt[sl][:]),
                        deps=[t_kt])
                    rPKT.read(sl, t_ke)
                    k_tok[tt] = t_ke

            def xnt_dest_1a(tt, g):
                sl = tt % 2
                return xnt_t[sl][:, g * 1024:(g + 1) * 1024], sl

            ln_pipeline(lambda tt: xkv[tt * 128:(tt + 1) * 128, :], NTT,
                        xnt_dest_1a, True)
            bld.flush()

        gate1b = bld.gate()
        xnt_slab = top.enter_context(
            nc.sbuf_tensor("xnt_slab", [128, NEC * 1024], BF16))
        slab3d = xnt_slab[:].rearrange("p (ec t) -> p ec t", ec=NEC)
        slab_tok = []

        with ExitStack() as sc1b:
            e1 = sc1b.enter_context
            x_t = [e1(nc.sbuf_tensor(f"xo_t{i}", [128, E], F32)) for i in range(2)]
            xn_t = [e1(nc.sbuf_tensor(f"xno_t{i}", [128, E], BF16)) for i in range(2)]
            bn_t = [e1(nc.sbuf_tensor(f"bno_t{i}", [128, 48], F32)) for i in range(2)]
            mv_t = [e1(nc.sbuf_tensor(f"mvo_t{i}", [128, 2], F32)) for i in range(2)]
            sd_t = [e1(nc.sbuf_tensor(f"sdo_t{i}", [128, 1], F32)) for i in range(2)]
            rs_t = [e1(nc.sbuf_tensor(f"rso_t{i}", [128, 1], F32)) for i in range(2)]
            nm_t = [e1(nc.sbuf_tensor(f"nmo_t{i}", [128, 1], F32)) for i in range(2)]
            ps_tr = [e1(nc.psum_tensor(f"ps_tro{i}", [128, 1024], BF16)) for i in range(2)]

            rX, rXN = Ring(2), Ring(2)
            rBN, rMV, rSD, rRS, rNM, rPTR = (Ring(2) for _ in range(6))

            for ot in range(NOWN):
                sl = ot % 2
                t_x = op("gp", lambda g, ot=ot, sl=sl: g.dma_start(
                    out=x_t[sl][:], in_=xown[ot]), deps=rX.war(sl) + gate1b)
                rX.wrote(sl, t_x)
                stat_toks = []
                for i in range(8):
                    t = op("dve", lambda v, sl=sl, i=i: v.bn_stats(
                        bn_t[sl][:, i * 6:(i + 1) * 6],
                        x_t[sl][:, i * 512:(i + 1) * 512]),
                        deps=[t_x] + (rBN.war(sl) if i == 0 else []) + gate1b)
                    stat_toks.append(t)
                    rX.read(sl, t)
                rBN.wrote(sl, stat_toks[-1])
                t_mv = op("dve", lambda v, sl=sl: v.bn_aggr(
                    mv_t[sl][:], bn_t[sl][:]), deps=stat_toks + rMV.war(sl))
                rMV.wrote(sl, t_mv)
                rBN.read(sl, t_mv)
                t_sd = op("act", lambda s, sl=sl: s.activation(
                    sd_t[sl][:], mv_t[sl][:, 1:2], AF.Sqrt,
                    bias=msc_sb[:, 0:1]), deps=[t_mv] + rSD.war(sl) + gate1b)
                rSD.wrote(sl, t_sd)
                t_rs = op("dve", lambda v, sl=sl: v.reciprocal(
                    rs_t[sl][:], sd_t[sl][:]), deps=[t_sd] + rRS.war(sl))
                rRS.wrote(sl, t_rs)
                rSD.read(sl, t_rs)
                t_nm = op("dve", lambda v, sl=sl: v.tensor_scalar(
                    nm_t[sl][:], mv_t[sl][:, 0:1], rs_t[sl][:, 0:1], -1.0,
                    ALU.mult, ALU.mult), deps=[t_rs] + rNM.war(sl))
                rNM.wrote(sl, t_nm)
                rMV.read(sl, t_nm)
                t_xn = op("act", lambda s, sl=sl: s.activation(
                    xn_t[sl][:], x_t[sl][:], AF.Identity,
                    bias=nm_t[sl][:, 0:1], scale=rs_t[sl][:, 0:1]),
                    deps=[t_nm, t_x] + rXN.war(sl))
                rXN.wrote(sl, t_xn)
                rX.read(sl, t_xn)
                rRS.read(sl, t_xn)
                rNM.read(sl, t_xn)
                for g in range(4):
                    ps = g % 2
                    tr_tok = None
                    war = rPTR.war(ps)
                    for j in range(8):
                        ec = g * 8 + j
                        tr_tok = op("pe", lambda t, sl=sl, ps=ps, j=j, ec=ec:
                                    t.transpose(ps_tr[ps][:, j * 128:(j + 1) * 128],
                                                xn_t[sl][:, ec * 128:(ec + 1) * 128],
                                                id_bf[:]),
                                    deps=[t_xn, t_id] + (war if j == 0 else []),
                                    inc=(j == 7))
                    rPTR.wrote(ps, tr_tok)
                    rXN.read(sl, tr_tok)
                    dst = slab3d[:, g * 8:(g + 1) * 8, ot * 128:(ot + 1) * 128]
                    src = ps_tr[ps][:].rearrange("p (a t) -> p a t", a=8)
                    t_ev = op("dve", lambda v, dst=dst, src=src: v.tensor_copy(
                        dst, src), deps=[tr_tok])
                    rPTR.read(ps, t_ev)
                    slab_tok.append(t_ev)
            bld.flush()

        q_sb = top.enter_context(nc.sbuf_tensor("q_sb", [128, H * 1024], BF16))
        q_tok = [None] * H

        with ExitStack() as sc1c:
            e1 = sc1c.enter_context
            wq_t = [e1(nc.sbuf_tensor(f"wq_t{i}", [128, NEC * 128], BF16)) for i in range(2)]
            ps_q = [e1(nc.psum_tensor(f"ps_q{i}", [128, 512], F32)) for i in range(2)]
            rWQ, rPQ = Ring(2), Ring(2)
            gate1c = bld.gate()
            for h in range(H):
                sl = h % 2
                src = wq[h].rearrange("(ec p) f -> p ec f", p=128)
                dst = wq_t[sl][:].rearrange("p (ec f) -> p ec f", ec=NEC)
                t_w = op("gp", lambda g, src=src, dst=dst: g.dma_start(
                    out=dst, in_=src, max_dma_last_dim=8192),
                    deps=rWQ.war(sl) + gate1c)
                rWQ.wrote(sl, t_w)
                for th in range(2):
                    mm_tok = None
                    war = rPQ.war(th)
                    for ec in range(NEC):
                        mm_tok = op("pe", lambda t, sl=sl, th=th, ec=ec: t.matmul(
                            ps_q[th][:],
                            wq_t[sl][:, ec * 128:(ec + 1) * 128],
                            xnt_slab[:, ec * 1024 + th * 512: ec * 1024 + th * 512 + 512],
                            start=(ec == 0), stop=(ec == NEC - 1)),
                            deps=[t_w] + slab_tok + (war if ec == 0 else []),
                            inc=(ec == NEC - 1))
                    rPQ.wrote(th, mm_tok)
                    rWQ.read(sl, mm_tok)
                    t_qe = op("dve", lambda v, h=h, th=th: v.tensor_copy(
                        q_sb[:, h * 1024 + th * 512: h * 1024 + th * 512 + 512],
                        ps_q[th][:]), deps=[mm_tok])
                    rPQ.read(th, t_qe)
                    q_tok[h] = t_qe
            bld.flush()

        if debug:
            g0 = bld.gate()
            op("sync", lambda s: s.dma_start(out=dbg_xnt, in_=xnt_slab[:]), deps=g0)
            op("sync", lambda s: s.dma_start(out=dbg_q, in_=q_sb[:]), deps=g0)
            op("sync", lambda s: s.dma_start(out=dbg_k, in_=k_fm[:]), deps=g0)
            op("sync", lambda s: s.dma_start(out=dbg_v, in_=v_tm[:]), deps=g0)
            bld.flush()

        attn_wr = [None] * H
        with ExitStack() as sc2:
            e2 = sc2.enter_context
            P_u = [e2(nc.sbuf_tensor(f"P_u{i}", [128, S], BF16)) for i in range(2)]
            vscr = [e2(nc.sbuf_tensor(f"vscr{i}", [128, 512], F32)) for i in range(2)]
            pt_sb = [e2(nc.sbuf_tensor(f"pt_sb{i}", [128, 1024], BF16)) for i in range(2)]
            sums = [e2(nc.sbuf_tensor(f"sums{i}", [128, 8], F32)) for i in range(2)]
            stot = [e2(nc.sbuf_tensor(f"stot{i}", [128, 1], F32)) for i in range(2)]
            rsum = [e2(nc.sbuf_tensor(f"rsum{i}", [128, 1], F32)) for i in range(2)]
            attn_t = [e2(nc.sbuf_tensor(f"attn_t{i}", [128, 128], BF16)) for i in range(2)]
            ps_s = [e2(nc.psum_tensor(f"ps_s{i}", [128, 512], F32)) for i in range(2)]
            ps_pt = [e2(nc.psum_tensor(f"ps_pt{i}", [128, 1024], BF16)) for i in range(2)]
            ps_av = [e2(nc.psum_tensor(f"ps_av{i}", [128, 128], F32)) for i in range(2)]

            rS, rV, rP, rSums, rStot, rRsum = (Ring(2) for _ in range(6))
            rPT, rPTS, rAV, rAT = (Ring(2) for _ in range(4))

            gate2 = bld.gate()
            units = [(d, h) for d in range(1, NOWN + 1) for h in range(H)]

            def emit_A(ui):
                d, h = units[ui]
                u = ui % 2
                exp_toks = []
                war_p = rP.war(u)
                war_sums = rSums.war(u)
                for kt in range(d):
                    m = d - 1 - kt
                    sl = kt % 2
                    t_s = op("pe", lambda t, d=d, h=h, kt=kt, sl=sl: t.matmul(
                        ps_s[sl][:],
                        q_sb[:, h * 1024 + (d - 1) * 128: h * 1024 + d * 128],
                        k_fm[:, kt * 512:(kt + 1) * 512],
                        start=True, stop=True),
                        deps=[q_tok[h]] + [k_tok[i] for i in range(kt * 4, kt * 4 + 4)]
                             + rS.war(sl) + gate2)
                    rS.wrote(sl, t_s)
                    dmoff = 512 if m == 0 else 0
                    t_v = op("dve", lambda v, sl=sl, dmoff=dmoff, h=h: v.scalar_tensor_tensor(
                        vscr[sl][:], cst_dm[:, dmoff:dmoff + 512],
                        float(slopes[h]), ps_s[sl][:], ALU.mult, ALU.add),
                        deps=[t_s, t_dm] + rV.war(sl))
                    rV.wrote(sl, t_v)
                    rS.read(sl, t_v)
                    bcol = h * 8 + m
                    t_e = op("act", lambda s, u=u, kt=kt, sl=sl, bcol=bcol: s.activation(
                        P_u[u][:, kt * 512:(kt + 1) * 512], vscr[sl][:], AF.Exp,
                        bias=bias_sb[:, bcol:bcol + 1],
                        accum_out=sums[u][:, kt:kt + 1]),
                        deps=[t_v, t_bias] + (war_p + war_sums if kt == 0 else []))
                    rV.read(sl, t_e)
                    exp_toks.append(t_e)
                rP.wrote(u, exp_toks[-1])
                rSums.wrote(u, exp_toks[-1])
                return exp_toks

            def emit_B(ui, exp_toks):
                d, h = units[ui]
                u = ui % 2
                t_red = op("dve", lambda v, u=u, d=d: v.reduce_sum(
                    stot[u][:], sums[u][:, 0:d], mybir.AxisListType.X),
                    deps=exp_toks + rStot.war(u))
                rStot.wrote(u, t_red)
                rSums.read(u, t_red)
                t_red2 = op("dve", lambda v, u=u: v.tensor_scalar(
                    stot[u][:], stot[u][:], 1e-30, None, ALU.add), deps=[t_red])
                rStot.wrote(u, t_red2)
                t_rc = op("dve", lambda v, u=u: v.reciprocal(
                    rsum[u][:], stot[u][:]), deps=[t_red2] + rRsum.war(u))
                rRsum.wrote(u, t_rc)
                rStot.read(u, t_rc)
                t_nr = op("dve", lambda v, u=u, d=d: v.tensor_scalar(
                    P_u[u][:, 0:d * 512], P_u[u][:, 0:d * 512],
                    rsum[u][:, 0:1], None, ALU.mult), deps=[t_rc])
                rRsum.read(u, t_nr)
                rP.wrote(u, t_nr)
                nch = d * 4
                av_tok = None
                war_av = rAV.war(u)
                for bank0 in range(0, nch, 8):
                    bp = (bank0 // 8) % 2
                    nb = min(8, nch - bank0)
                    tr_tok = None
                    war_pt = rPT.war(bp)
                    for j in range(nb):
                        ch = bank0 + j
                        tr_tok = op("pe", lambda t, bp=bp, j=j, u=u, ch=ch:
                                    t.transpose(ps_pt[bp][:, j * 128:(j + 1) * 128],
                                                P_u[u][:, ch * 128:(ch + 1) * 128],
                                                id_bf[:]),
                                    deps=[t_nr, t_id] + (war_pt if j == 0 else []),
                                    inc=(j == nb - 1))
                    rPT.wrote(bp, tr_tok)
                    rP.read(u, tr_tok)
                    t_pe = op("act", lambda s, bp=bp, nb=nb: s.copy(
                        pt_sb[bp][:, 0:nb * 128], ps_pt[bp][:, 0:nb * 128]),
                        deps=[tr_tok] + rPTS.war(bp))
                    rPTS.wrote(bp, t_pe)
                    rPT.read(bp, t_pe)
                    for j in range(nb):
                        ch = bank0 + j
                        av_tok = op("pe", lambda t, u=u, bp=bp, j=j, ch=ch, first=(ch == 0), last=(ch == nch - 1): t.matmul(
                            ps_av[u][:],
                            v_tm[:, ch * 128:(ch + 1) * 128],
                            pt_sb[bp][:, j * 128:(j + 1) * 128],
                            start=first, stop=last),
                            deps=[t_pe, kv_tok[ch]] + (war_av if ch == 0 else []),
                            inc=(j == nb - 1))
                    rPTS.read(bp, av_tok)
                rAV.wrote(u, av_tok)
                t_ae = op("dve", lambda v, u=u: v.tensor_copy(
                    attn_t[u][:], ps_av[u][:]), deps=[av_tok] + rAT.war(u))
                rAT.wrote(u, t_ae)
                rAV.read(u, t_ae)
                t_aw = op("sync", lambda s, u=u, h=h, d=d: s.dma_start(
                    out=attn_d[h, :, (d - 1) * 128:d * 128], in_=attn_t[u][:]),
                    deps=[t_ae])
                rAT.read(u, t_aw)
                attn_wr[h] = t_aw

            prev = None
            for ui in range(len(units)):
                toks = emit_A(ui)
                if prev is not None:
                    emit_B(prev[0], prev[1])
                prev = (ui, toks)
            emit_B(prev[0], prev[1])
            bld.flush()

        s_wr = {}
        with ExitStack() as scw:
            e2 = scw.enter_context
            wc_t = [e2(nc.sbuf_tensor(f"wc_t{i}", [128, 512], BF16)) for i in range(2)]
            at_t = [e2(nc.sbuf_tensor(f"at_t{i}", [128, 1024], BF16)) for i in range(2)]
            s_t = [e2(nc.sbuf_tensor(f"s_t{i}", [128, 512], F32)) for i in range(2)]
            ps_wc = [e2(nc.psum_tensor(f"ps_wc{i}", [128, 512], F32)) for i in range(8)]
            rWC, rAT2, rST = Ring(2), Ring(2), Ring(2)
            rPW = Ring(8)
            gatew = bld.gate()
            for ep in range(NEP):
                stop_toks = [None] * 8
                for hc in range(H):
                    sl = hc % 2
                    t_w = op("gp", lambda g, hc=hc, ep=ep, sl=sl: g.dma_start(
                        out=wc_t[sl][:],
                        in_=wc[hc * 128:(hc + 1) * 128, ep * 512:(ep + 1) * 512]),
                        deps=rWC.war(sl) + gatew)
                    rWC.wrote(sl, t_w)
                    t_a = op("sync", lambda s, hc=hc, sl=sl: s.dma_start(
                        out=at_t[sl][:], in_=attn_d[hc]),
                        deps=[attn_wr[hc]] + rAT2.war(sl) + gatew)
                    rAT2.wrote(sl, t_a)
                    for dd in range(8):
                        war = rPW.war(dd) if hc == 0 else []
                        tok = op("pe", lambda t, sl=sl, dd=dd, hc=hc: t.matmul(
                            ps_wc[dd][:],
                            at_t[sl][:, dd * 128:(dd + 1) * 128],
                            wc_t[sl][:],
                            start=(hc == 0), stop=(hc == H - 1)),
                            deps=[t_w, t_a] + war,
                            inc=(hc == H - 1 or dd == 7))
                        if dd == 7:
                            rWC.read(sl, tok)
                            rAT2.read(sl, tok)
                        if hc == H - 1:
                            stop_toks[dd] = tok
                            rPW.wrote(dd, tok)
                for dd in range(8):
                    sl = dd % 2
                    t_se = op("act", lambda s, dd=dd, sl=sl: s.copy(
                        s_t[sl][:], ps_wc[dd][:]),
                        deps=[stop_toks[dd]] + rST.war(sl))
                    rST.wrote(sl, t_se)
                    rPW.read(dd, t_se)
                    t_sw = op("sync", lambda s, dd=dd, ep=ep, sl=sl: s.dma_start(
                        out=s_d[dd * 128:(dd + 1) * 128, ep * 512:(ep + 1) * 512],
                        in_=s_t[sl][:]), deps=[t_se])
                    rST.read(sl, t_sw)
                    s_wr[(dd, ep)] = t_sw
            bld.flush()

        h_wr = [None] * NFC
        with ExitStack() as s3a:
            e3 = s3a.enter_context
            wg_t = [e3(nc.sbuf_tensor(f"wg_t{i}", [128, NEC * 128], BF16)) for i in range(2)]
            wu_t = [e3(nc.sbuf_tensor(f"wu_t{i}", [128, NEC * 128], BF16)) for i in range(2)]
            sg_t = [e3(nc.sbuf_tensor(f"sg_t{i}", [128, 512], F32)) for i in range(2)]
            h_t = [e3(nc.sbuf_tensor(f"h_t{i}", [128, 1024], BF16)) for i in range(2)]
            ps_f = [e3(nc.psum_tensor(f"ps_f{i}", [128, 512], F32)) for i in range(8)]
            rWG, rWU, rSG, rHT = Ring(2), Ring(2), Ring(2), Ring(2)
            rPF = Ring(8)
            gate3 = bld.gate()
            for fc in range(NFC):
                sl = fc % 2
                srcg = wg[fc]
                dstg = wg_t[sl][:].rearrange("p (ec f) -> p ec f", ec=NEC)
                t_wg = op("gp", lambda g, srcg=srcg, dstg=dstg: g.dma_start(
                    out=dstg, in_=srcg, max_dma_last_dim=8192),
                    deps=rWG.war(sl) + gate3)
                rWG.wrote(sl, t_wg)
                srcu = wu[fc]
                dstu = wu_t[sl][:].rearrange("p (ec f) -> p ec f", ec=NEC)
                t_wu = op("gp", lambda g, srcu=srcu, dstu=dstu: g.dma_start(
                    out=dstu, in_=srcu, max_dma_last_dim=8192),
                    deps=rWU.war(sl) + gate3)
                rWU.wrote(sl, t_wu)
                hm_toks = []
                for th in range(2):
                    bg = sl * 4 + th * 2
                    bu = bg + 1
                    war_g = rPF.war(bg)
                    war_u = rPF.war(bu)
                    g_tok = None
                    u_tok = None
                    for ec in range(NEC):
                        g_tok = op("pe", lambda t, sl=sl, th=th, ec=ec, bg=bg: t.matmul(
                            ps_f[bg][:],
                            wg_t[sl][:, ec * 128:(ec + 1) * 128],
                            xnt_slab[:, ec * 1024 + th * 512: ec * 1024 + th * 512 + 512],
                            start=(ec == 0), stop=(ec == NEC - 1)),
                            deps=[t_wg] + slab_tok + (war_g if ec == 0 else []),
                            inc=(ec == NEC - 1))
                    for ec in range(NEC):
                        u_tok = op("pe", lambda t, sl=sl, th=th, ec=ec, bu=bu: t.matmul(
                            ps_f[bu][:],
                            wu_t[sl][:, ec * 128:(ec + 1) * 128],
                            xnt_slab[:, ec * 1024 + th * 512: ec * 1024 + th * 512 + 512],
                            start=(ec == 0), stop=(ec == NEC - 1)),
                            deps=[t_wu] + slab_tok + (war_u if ec == 0 else []),
                            inc=(ec == NEC - 1))
                    rPF.wrote(bg, g_tok)
                    rPF.wrote(bu, u_tok)
                    rWG.read(sl, g_tok)
                    rWU.read(sl, u_tok)
                    ssl = th
                    t_sg = op("act", lambda s, bg=bg, ssl=ssl: s.activation(
                        sg_t[ssl][:], ps_f[bg][:], AF.Silu),
                        deps=[g_tok] + rSG.war(ssl))
                    rSG.wrote(ssl, t_sg)
                    rPF.read(bg, t_sg)
                    war_h = rHT.war(sl) if th == 0 else []
                    t_hm = op("dve", lambda v, sl=sl, th=th, ssl=ssl, bu=bu: v.tensor_tensor(
                        h_t[sl][:, th * 512:(th + 1) * 512], sg_t[ssl][:],
                        ps_f[bu][:], ALU.mult), deps=[t_sg, u_tok] + war_h)
                    rSG.read(ssl, t_hm)
                    rPF.read(bu, t_hm)
                    hm_toks.append(t_hm)
                rHT.wrote(sl, hm_toks[-1])
                t_hw = op("sync", lambda s, fc=fc, sl=sl: s.dma_start(
                    out=h_d[fc], in_=h_t[sl][:]), deps=hm_toks)
                rHT.read(sl, t_hw)
                h_wr[fc] = t_hw
            bld.flush()

        with ExitStack() as s3b:
            e3 = s3b.enter_context
            h2_t = [e3(nc.sbuf_tensor(f"h2_t{i}", [128, 1024], BF16)) for i in range(2)]
            wd_t = [e3(nc.sbuf_tensor(f"wd_t{i}", [128, 512], BF16)) for i in range(2)]
            s_sl = [e3(nc.sbuf_tensor(f"s_sl{i}", [128, 512], F32)) for i in range(8)]
            x_sl = [e3(nc.sbuf_tensor(f"x_sl{i}", [128, 512], F32)) for i in range(8)]
            g_t = [e3(nc.sbuf_tensor(f"g_t{i}", [128, 512], F32)) for i in range(2)]
            o_t = [e3(nc.sbuf_tensor(f"o_t{i}", [128, 512], F32)) for i in range(2)]
            o2_t = [e3(nc.sbuf_tensor(f"o2_t{i}", [128, 512], F32)) for i in range(2)]
            ps_y = [e3(nc.psum_tensor(f"ps_y{i}", [128, 512], F32)) for i in range(8)]
            rH2, rWD, rG, rO, rO2 = Ring(2), Ring(2), Ring(2), Ring(2), Ring(2)
            rSL, rXL = Ring(8), Ring(8)
            rPY = Ring(8)
            gate3b = bld.gate()
            for ep in range(NEP):
                sx_toks = []
                for tt in range(8):
                    t_sx = op("gp", lambda g, tt=tt, ep=ep: g.dma_start(
                        out=s_sl[tt][:],
                        in_=s_d[tt * 128:(tt + 1) * 128, ep * 512:(ep + 1) * 512]),
                        deps=[s_wr[(tt, ep)]] + rSL.war(tt) + gate3b)
                    rSL.wrote(tt, t_sx)
                    t_xx = op("gp", lambda g, tt=tt, ep=ep: g.dma_start(
                        out=x_sl[tt][:],
                        in_=xown[tt][:, ep * 512:(ep + 1) * 512]),
                        deps=rXL.war(tt) + gate3b)
                    rXL.wrote(tt, t_xx)
                    sx_toks.append((t_sx, t_xx))
                stop_toks = [None] * 8
                for fc in range(NFC):
                    sl = fc % 2
                    t_h = op("sync", lambda s, fc=fc, sl=sl: s.dma_start(
                        out=h2_t[sl][:], in_=h_d[fc]),
                        deps=[h_wr[fc]] + rH2.war(sl) + gate3b)
                    rH2.wrote(sl, t_h)
                    t_w = op("gp", lambda g, ep=ep, fc=fc, sl=sl: g.dma_start(
                        out=wd_t[sl][:], in_=wd[ep, fc]),
                        deps=rWD.war(sl) + gate3b)
                    rWD.wrote(sl, t_w)
                    for tt in range(8):
                        war = rPY.war(tt) if fc == 0 else []
                        tok = op("pe", lambda t, sl=sl, tt=tt, fc=fc: t.matmul(
                            ps_y[tt][:],
                            h2_t[sl][:, tt * 128:(tt + 1) * 128],
                            wd_t[sl][:],
                            start=(fc == 0), stop=(fc == NFC - 1)),
                            deps=[t_h, t_w] + war,
                            inc=(fc == NFC - 1 or tt == 7))
                        if tt == 7:
                            rH2.read(sl, tok)
                            rWD.read(sl, tok)
                        if fc == NFC - 1:
                            stop_toks[tt] = tok
                            rPY.wrote(tt, tok)
                for tt in range(8):
                    sl = tt % 2
                    t_g = op("act", lambda s, tt=tt, sl=sl: s.activation(
                        g_t[sl][:], ps_y[tt][:], AF.Silu),
                        deps=[stop_toks[tt]] + rG.war(sl))
                    rG.wrote(sl, t_g)
                    rPY.read(tt, t_g)
                    t_m = op("dve", lambda v, tt=tt, sl=sl: v.tensor_mul(
                        o_t[sl][:], s_sl[tt][:], g_t[sl][:]),
                        deps=[t_g, sx_toks[tt][0]] + rO.war(sl))
                    rO.wrote(sl, t_m)
                    rG.read(sl, t_m)
                    rSL.read(tt, t_m)
                    t_a = op("dve", lambda v, tt=tt, sl=sl: v.tensor_add(
                        o2_t[sl][:], o_t[sl][:], x_sl[tt][:]),
                        deps=[t_m, sx_toks[tt][1]] + rO2.war(sl))
                    rO2.wrote(sl, t_a)
                    rO.read(sl, t_a)
                    rXL.read(tt, t_a)
                    t_ow = op("sync", lambda s, tt=tt, ep=ep, sl=sl: s.dma_start(
                        out=out[tt][:, ep * 512:(ep + 1) * 512], in_=o2_t[sl][:]),
                        deps=[t_a])
                    rO2.read(sl, t_ow)
            final = bld.count["sync"]
            fsem = bld.sems["sync"]
            bld.raw("sync", lambda s: s.wait_ge(fsem, final))
            bld.flush()

    return nc


def _prep_shared(w_qkv, w_c, W_gate, W_up, W_down):
    wkv = np.ascontiguousarray(w_qkv[:, E:E + 256])
    wqp = np.ascontiguousarray(
        w_qkv[:, :E].reshape(E, H, 128).transpose(1, 0, 2))
    wgp = np.ascontiguousarray(
        W_gate.reshape(NFC, 128, NEC, 128).transpose(0, 3, 2, 1))
    wup = np.ascontiguousarray(
        W_up.reshape(NFC, 128, NEC, 128).transpose(0, 3, 2, 1))
    wdp = np.ascontiguousarray(
        W_down.reshape(NEP, 512, NFC, 128).transpose(0, 2, 3, 1))
    return wkv, wqp, np.ascontiguousarray(w_c), wgp, wup, wdp


def _consts_for_slot(slot):
    slopes = _alibi_slopes()
    j = np.arange(512, dtype=np.float64)[None, :]
    i = np.arange(128, dtype=np.float64)[:, None]
    gen = (j - i).astype(np.float32)
    diag = gen.copy()
    diag[(j - i) > 128 * slot] = MASK_NEG
    dm = np.concatenate([gen, diag], axis=1).astype(np.float32)
    bias = np.zeros((128, H * 8), dtype=np.float32)
    for h in range(H):
        for m in range(8):
            delta = -512.0 * m - 128.0 * slot
            bias[:, h * 8 + m] = np.float32(-BSTAT + float(slopes[h]) * delta)
    ident = np.eye(128, dtype=np.float32)
    msc = np.zeros((128, 8), dtype=np.float32)
    msc[:, 0] = 1e-5
    return dm, bias, ident, msc


def kernel(L0_attn_hidden_states, L0_attn_w_qkv, L0_attn_w_c,
           L0_ffn_W_gate, L0_ffn_W_up, L0_ffn_W_down, _debug=False):
    x = np.asarray(L0_attn_hidden_states, dtype=np.float32)
    wkv, wqp, wcc, wgp, wup, wdp = _prep_shared(
        np.asarray(L0_attn_w_qkv, dtype=np.float32),
        np.asarray(L0_attn_w_c, dtype=np.float32),
        np.asarray(L0_ffn_W_gate, dtype=np.float32),
        np.asarray(L0_ffn_W_up, dtype=np.float32),
        np.asarray(L0_ffn_W_down, dtype=np.float32))

    nc = build_nc(debug=_debug)
    in_maps = []
    for core in range(8):
        b, slot = core // 4, core % 4
        dm, bias, ident, msc = _consts_for_slot(slot)
        xb = np.ascontiguousarray(x[b])
        xown = np.ascontiguousarray(
            xb.reshape(8, 4, 128, E)[:, slot])
        in_maps.append({
            "xkv": xb, "xown": xown, "wkv": wkv, "wq": wqp, "wc": wcc,
            "wg": wgp, "wu": wup, "wd": wdp,
            "dmc": dm, "biasc": bias, "identc": ident, "mscc": msc,
        })
    res = run_bass_kernel_spmd(nc, in_maps, core_ids=list(range(8)))
    full = np.empty((B, S, E), dtype=np.float32)
    for core in range(8):
        b, slot = core // 4, core % 4
        o = np.asarray(res.results[core]["out"])
        full[b].reshape(8, 4, 128, E)[:, slot] = o
    if _debug:
        return full, res
    return full


# version 02
# speedup vs baseline: 1.1505x; 1.1505x over previous
import math
from contextlib import ExitStack

import numpy as np
import ml_dtypes
import concourse.bass as bass
import concourse.mybir as mybir
from concourse.bass_utils import run_bass_kernel_spmd

F32 = mybir.dt.float32
BF16 = mybir.dt.bfloat16
AF = mybir.ActivationFunctionType
ALU = mybir.AluOpType

B, S, E, H, DH, FF = 2, 4096, 4096, 32, 128, 11008
NEC = E // 128
NFC = FF // 128
NTT = S // 128
NOWN = 8
NEP = E // 512
SCALING = float(128 ** -0.5)
BSTAT = 30.0
MASK_NEG = -1.0e36


def _alibi_slopes():
    base = 2.0 ** (-(2.0 ** -(math.log2(32) - 3)))
    return (base ** np.arange(1, 33, dtype=np.float64)).astype(np.float32)


class _Op:
    __slots__ = ("q", "waits", "fn", "inc")

    def __init__(self, q, waits, fn, inc):
        self.q, self.waits, self.fn, self.inc = q, waits, fn, inc


class Builder:
    QUEUES = {"pe": 1, "dve": 1, "act": 1, "sync": 16, "gp": 16}
    ENG = {"pe": "tensor", "dve": "vector", "act": "scalar",
           "sync": "sync", "gp": "gpsimd"}

    def __init__(self, nc, stack):
        self.nc = nc
        self.sems = {q: stack.enter_context(nc.semaphore(f"ctr_{q}"))
                     for q in self.QUEUES}
        self.count = {q: 0 for q in self.QUEUES}
        self.hw = {q: {} for q in self.QUEUES}
        self.pending = []

    def op(self, q, fn, deps=(), inc=True):
        merged = {}
        for tok in deps:
            if tok is None:
                continue
            dq, dv = tok
            if dq == q and q == "pe":
                continue
            if merged.get(dq, 0) < dv:
                merged[dq] = dv
        hw = self.hw[q]
        waits = []
        for dq, dv in merged.items():
            if hw.get(dq, 0) >= dv:
                continue
            waits.append((self.sems[dq], dv))
            hw[dq] = dv
        qty = self.QUEUES[q] if inc else 0
        tok = None
        if inc:
            self.count[q] += qty
            tok = (q, self.count[q])
        self.pending.append(_Op(q, waits, fn, qty))
        return tok

    def gate(self):
        return [(q, n) for q, n in self.count.items() if n > 0]

    def raw(self, q, fn):
        self.pending.append(_Op(q, [], fn, 0))

    def flush(self):
        ops = self.pending
        self.pending = []
        if not ops:
            return
        with self.nc.Block() as blk:
            for q, eng_attr in self.ENG.items():
                qops = [o for o in ops if o.q == q]
                if not qops:
                    continue

                def emit(eng, qops=qops, q=q):
                    for o in qops:
                        for sem, val in o.waits:
                            eng.wait_ge(sem, val)
                        inst = o.fn(eng)
                        if o.inc:
                            inst.then_inc(self.sems[q], o.inc)

                getattr(blk, eng_attr)(emit)


class Ring:

    def __init__(self, n):
        self.n = n
        self.wr = [None] * n
        self.rd = [[] for _ in range(n)]

    def war(self, s):
        deps = list(self.rd[s])
        self.rd[s] = []
        if self.wr[s] is not None:
            deps.append(self.wr[s])
        return deps

    def wrote(self, s, tok):
        self.wr[s] = tok

    def read(self, s, tok):
        if tok is not None:
            self.rd[s].append(tok)


def build_nc(debug=False):
    nc = bass.Bass("TRN2", target_bir_lowering=False)

    xkv = nc.dram_tensor("xkv", [S, E], F32, kind="ExternalInput").ap()
    xown = nc.dram_tensor("xown", [NOWN, 128, E], F32, kind="ExternalInput").ap()
    wkv = nc.dram_tensor("wkv", [E, 256], F32, kind="ExternalInput").ap()
    wq = nc.dram_tensor("wq", [H, E, 128], F32, kind="ExternalInput").ap()
    wc = nc.dram_tensor("wc", [E, E], F32, kind="ExternalInput").ap()
    wg = nc.dram_tensor("wg", [NFC, 128, NEC, 128], F32, kind="ExternalInput").ap()
    wu = nc.dram_tensor("wu", [NFC, 128, NEC, 128], F32, kind="ExternalInput").ap()
    wd = nc.dram_tensor("wd", [NEP, NFC, 128, 512], F32, kind="ExternalInput").ap()
    dmc = nc.dram_tensor("dmc", [128, 1024], F32, kind="ExternalInput").ap()
    biasc = nc.dram_tensor("biasc", [128, H * 8], F32, kind="ExternalInput").ap()
    identc = nc.dram_tensor("identc", [128, 128], F32, kind="ExternalInput").ap()
    mscc = nc.dram_tensor("mscc", [128, 8], F32, kind="ExternalInput").ap()
    out = nc.dram_tensor("out", [NOWN, 128, E], F32, kind="ExternalOutput").ap()

    skind = "ExternalOutput" if debug else "Internal"
    attn_d = nc.dram_tensor("attn_d", [H, 128, 1024], BF16, kind=skind).ap()
    s_d = nc.dram_tensor("s_d", [1024, E], F32, kind=skind).ap()
    h_d = nc.dram_tensor("h_d", [NFC, 128, 1024], BF16, kind=skind).ap()
    if debug:
        dbg_xnt = nc.dram_tensor("dbg_xnt", [128, NEC * 1024], BF16, kind="ExternalOutput").ap()
        dbg_q = nc.dram_tensor("dbg_q", [128, H * 1024], BF16, kind="ExternalOutput").ap()
        dbg_k = nc.dram_tensor("dbg_k", [128, S], BF16, kind="ExternalOutput").ap()
        dbg_v = nc.dram_tensor("dbg_v", [128, NTT * 128], BF16, kind="ExternalOutput").ap()

    slopes = _alibi_slopes()

    with ExitStack() as top:
        bld = Builder(nc, top)
        op = bld.op

        e = top.enter_context
        cst_dm = e(nc.sbuf_tensor("cst_dm", [128, 1024], F32))
        bias_sb = e(nc.sbuf_tensor("bias_sb", [128, H * 8], F32))
        msc_sb = e(nc.sbuf_tensor("msc_sb", [128, 8], F32))
        id_bf = e(nc.sbuf_tensor("id_bf", [128, 128], BF16))
        k_fm = e(nc.sbuf_tensor("k_fm", [128, S], BF16))
        v_tm = e(nc.sbuf_tensor("v_tm", [128, NTT * 128], BF16))

        t_dm = op("sync", lambda s: s.dma_start(out=cst_dm[:], in_=dmc))
        t_bias = op("sync", lambda s: s.dma_start(out=bias_sb[:], in_=biasc))
        t_msc = op("sync", lambda s: s.dma_start(out=msc_sb[:], in_=mscc))
        t_id = op("gp", lambda g: g.dma_start(out=id_bf[:], in_=identc))

        kv_tok = [None] * NTT
        k_tok = [None] * NTT

        with ExitStack() as sc1:
            e1 = sc1.enter_context
            x_t = [e1(nc.sbuf_tensor(f"x_t{i}", [128, E], F32)) for i in range(2)]
            xn_t = [e1(nc.sbuf_tensor(f"xn_t{i}", [128, E], BF16)) for i in range(2)]
            xnt_t = [e1(nc.sbuf_tensor(f"xnt_t{i}", [128, NEC * 128], BF16)) for i in range(2)]
            wkv_sb = e1(nc.sbuf_tensor("wkv_sb", [128, NEC * 256], BF16))
            bn_t = [e1(nc.sbuf_tensor(f"bn_t{i}", [128, 48], F32)) for i in range(2)]
            mv_t = [e1(nc.sbuf_tensor(f"mv_t{i}", [128, 2], F32)) for i in range(2)]
            sd_t = [e1(nc.sbuf_tensor(f"sd_t{i}", [128, 1], F32)) for i in range(2)]
            rs_t = [e1(nc.sbuf_tensor(f"rs_t{i}", [128, 1], F32)) for i in range(2)]
            nm_t = [e1(nc.sbuf_tensor(f"nm_t{i}", [128, 1], F32)) for i in range(2)]
            ktm_t = [e1(nc.sbuf_tensor(f"ktm_t{i}", [128, 128], BF16)) for i in range(2)]
            ps_tr = [e1(nc.psum_tensor(f"ps_tr{i}", [128, 1024], BF16)) for i in range(2)]
            ps_kv = [e1(nc.psum_tensor(f"ps_kv{i}", [128, 256], F32)) for i in range(2)]
            ps_kt = [e1(nc.psum_tensor(f"ps_kt{i}", [128, 128], BF16)) for i in range(2)]

            wkv_r = wkv.rearrange("(ec p) c -> p ec c", p=128)
            wkv_dst = wkv_sb[:].rearrange("p (ec c) -> p ec c", ec=NEC)
            t_wkv = op("gp", lambda g: g.dma_start(out=wkv_dst, in_=wkv_r,
                                                   max_dma_last_dim=8192))

            rX, rXN, rXNT = Ring(2), Ring(2), Ring(2)
            rBN, rMV, rSD, rRS, rNM = Ring(2), Ring(2), Ring(2), Ring(2), Ring(2)
            rKTM, rPTR, rPKV, rPKT = Ring(2), Ring(2), Ring(2), Ring(2)

            def ln_pipeline(src_ap_fn, n_tiles, xnt_dest_fn, kv_consume):
                for tt in range(n_tiles):
                    sl = tt % 2
                    t_x = op("gp", lambda g, tt=tt, sl=sl: g.dma_start(
                        out=x_t[sl][:], in_=src_ap_fn(tt)), deps=rX.war(sl))
                    rX.wrote(sl, t_x)
                    stat_toks = []
                    for i in range(8):
                        t = op("dve", lambda v, sl=sl, i=i: v.bn_stats(
                            bn_t[sl][:, i * 6:(i + 1) * 6],
                            x_t[sl][:, i * 512:(i + 1) * 512]),
                            deps=[t_x] + (rBN.war(sl) if i == 0 else []))
                        stat_toks.append(t)
                        rX.read(sl, t)
                    rBN.wrote(sl, stat_toks[-1])
                    t_mv = op("dve", lambda v, sl=sl: v.bn_aggr(
                        mv_t[sl][:], bn_t[sl][:]), deps=stat_toks + rMV.war(sl))
                    rMV.wrote(sl, t_mv)
                    rBN.read(sl, t_mv)
                    t_sd = op("act", lambda s, sl=sl: s.activation(
                        sd_t[sl][:], mv_t[sl][:, 1:2], AF.Sqrt,
                        bias=msc_sb[:, 0:1]), deps=[t_mv, t_msc] + rSD.war(sl))
                    rSD.wrote(sl, t_sd)
                    t_rs = op("dve", lambda v, sl=sl: v.reciprocal(
                        rs_t[sl][:], sd_t[sl][:]), deps=[t_sd] + rRS.war(sl))
                    rRS.wrote(sl, t_rs)
                    rSD.read(sl, t_rs)
                    t_nm = op("dve", lambda v, sl=sl: v.tensor_scalar(
                        nm_t[sl][:], mv_t[sl][:, 0:1], rs_t[sl][:, 0:1], -1.0,
                        ALU.mult, ALU.mult), deps=[t_rs] + rNM.war(sl))
                    rNM.wrote(sl, t_nm)
                    rMV.read(sl, t_nm)
                    t_xn = op("act", lambda s, sl=sl: s.activation(
                        xn_t[sl][:], x_t[sl][:], AF.Identity,
                        bias=nm_t[sl][:, 0:1], scale=rs_t[sl][:, 0:1]),
                        deps=[t_nm, t_x] + rXN.war(sl))
                    rXN.wrote(sl, t_xn)
                    rX.read(sl, t_xn)
                    rRS.read(sl, t_xn)
                    rNM.read(sl, t_xn)
                    ev_toks = []
                    for g in range(4):
                        ps = g % 2
                        tr_tok = None
                        war = rPTR.war(ps)
                        for j in range(8):
                            ec = g * 8 + j
                            tr_tok = op("pe", lambda t, sl=sl, ps=ps, j=j, ec=ec:
                                        t.transpose(ps_tr[ps][:, j * 128:(j + 1) * 128],
                                                    xn_t[sl][:, ec * 128:(ec + 1) * 128],
                                                    id_bf[:]),
                                        deps=[t_xn, t_id] + (war if j == 0 else []),
                                        inc=(j == 7))
                        rPTR.wrote(ps, tr_tok)
                        rXN.read(sl, tr_tok)
                        dst, dslot = xnt_dest_fn(tt, g)
                        dep_war = rXNT.war(dslot) if dslot is not None else []
                        t_ev = op("dve", lambda v, dst=dst, ps=ps: v.tensor_copy(
                            dst, ps_tr[ps][:]), deps=[tr_tok] + dep_war)
                        if dslot is not None:
                            rXNT.wrote(dslot, t_ev)
                        rPTR.read(ps, t_ev)
                        ev_toks.append(t_ev)
                    if not kv_consume:
                        continue
                    mm_tok = None
                    war = rPKV.war(sl)
                    for ec in range(NEC):
                        mm_tok = op("pe", lambda t, sl=sl, ec=ec: t.matmul(
                            ps_kv[sl][:],
                            xnt_t[sl][:, ec * 128:(ec + 1) * 128],
                            wkv_sb[:, ec * 256:(ec + 1) * 256],
                            start=(ec == 0), stop=(ec == NEC - 1)),
                            deps=[ev_toks[ec // 8], t_wkv] + (war if ec == 0 else []),
                            inc=(ec == NEC - 1))
                    rPKV.wrote(sl, mm_tok)
                    for t_ev in ev_toks:
                        rXNT.read(sl, mm_tok)
                    t_ks = op("dve", lambda v, sl=sl: v.tensor_scalar(
                        ktm_t[sl][:], ps_kv[sl][:, 0:128], SCALING, None,
                        ALU.mult), deps=[mm_tok] + rKTM.war(sl))
                    rKTM.wrote(sl, t_ks)
                    rPKV.read(sl, t_ks)
                    t_vc = op("dve", lambda v, sl=sl, tt=tt: v.tensor_copy(
                        v_tm[:, tt * 128:(tt + 1) * 128], ps_kv[sl][:, 128:256]),
                        deps=[mm_tok])
                    rPKV.read(sl, t_vc)
                    kv_tok[tt] = t_vc
                    t_kt = op("pe", lambda t, sl=sl: t.transpose(
                        ps_kt[sl][:], ktm_t[sl][:], id_bf[:]),
                        deps=[t_ks, t_id] + rPKT.war(sl))
                    rPKT.wrote(sl, t_kt)
                    rKTM.read(sl, t_kt)
                    t_ke = op("dve", lambda v, sl=sl, tt=tt: v.tensor_copy(
                        k_fm[:, tt * 128:(tt + 1) * 128], ps_kt[sl][:]),
                        deps=[t_kt])
                    rPKT.read(sl, t_ke)
                    k_tok[tt] = t_ke

            def xnt_dest_1a(tt, g):
                sl = tt % 2
                return xnt_t[sl][:, g * 1024:(g + 1) * 1024], sl

            ln_pipeline(lambda tt: xkv[tt * 128:(tt + 1) * 128, :], NTT,
                        xnt_dest_1a, True)
            bld.flush()

        gate1b = bld.gate()
        xnt_slab = top.enter_context(
            nc.sbuf_tensor("xnt_slab", [128, NEC * 1024], BF16))
        slab3d = xnt_slab[:].rearrange("p (ec t) -> p ec t", ec=NEC)
        slab_tok = []

        with ExitStack() as sc1b:
            e1 = sc1b.enter_context
            x_t = [e1(nc.sbuf_tensor(f"xo_t{i}", [128, E], F32)) for i in range(2)]
            xn_t = [e1(nc.sbuf_tensor(f"xno_t{i}", [128, E], BF16)) for i in range(2)]
            bn_t = [e1(nc.sbuf_tensor(f"bno_t{i}", [128, 48], F32)) for i in range(2)]
            mv_t = [e1(nc.sbuf_tensor(f"mvo_t{i}", [128, 2], F32)) for i in range(2)]
            sd_t = [e1(nc.sbuf_tensor(f"sdo_t{i}", [128, 1], F32)) for i in range(2)]
            rs_t = [e1(nc.sbuf_tensor(f"rso_t{i}", [128, 1], F32)) for i in range(2)]
            nm_t = [e1(nc.sbuf_tensor(f"nmo_t{i}", [128, 1], F32)) for i in range(2)]
            ps_tr = [e1(nc.psum_tensor(f"ps_tro{i}", [128, 1024], BF16)) for i in range(2)]

            rX, rXN = Ring(2), Ring(2)
            rBN, rMV, rSD, rRS, rNM, rPTR = (Ring(2) for _ in range(6))

            for ot in range(NOWN):
                sl = ot % 2
                t_x = op("gp", lambda g, ot=ot, sl=sl: g.dma_start(
                    out=x_t[sl][:], in_=xown[ot]), deps=rX.war(sl) + gate1b)
                rX.wrote(sl, t_x)
                stat_toks = []
                for i in range(8):
                    t = op("dve", lambda v, sl=sl, i=i: v.bn_stats(
                        bn_t[sl][:, i * 6:(i + 1) * 6],
                        x_t[sl][:, i * 512:(i + 1) * 512]),
                        deps=[t_x] + (rBN.war(sl) if i == 0 else []) + gate1b)
                    stat_toks.append(t)
                    rX.read(sl, t)
                rBN.wrote(sl, stat_toks[-1])
                t_mv = op("dve", lambda v, sl=sl: v.bn_aggr(
                    mv_t[sl][:], bn_t[sl][:]), deps=stat_toks + rMV.war(sl))
                rMV.wrote(sl, t_mv)
                rBN.read(sl, t_mv)
                t_sd = op("act", lambda s, sl=sl: s.activation(
                    sd_t[sl][:], mv_t[sl][:, 1:2], AF.Sqrt,
                    bias=msc_sb[:, 0:1]), deps=[t_mv] + rSD.war(sl) + gate1b)
                rSD.wrote(sl, t_sd)
                t_rs = op("dve", lambda v, sl=sl: v.reciprocal(
                    rs_t[sl][:], sd_t[sl][:]), deps=[t_sd] + rRS.war(sl))
                rRS.wrote(sl, t_rs)
                rSD.read(sl, t_rs)
                t_nm = op("dve", lambda v, sl=sl: v.tensor_scalar(
                    nm_t[sl][:], mv_t[sl][:, 0:1], rs_t[sl][:, 0:1], -1.0,
                    ALU.mult, ALU.mult), deps=[t_rs] + rNM.war(sl))
                rNM.wrote(sl, t_nm)
                rMV.read(sl, t_nm)
                t_xn = op("act", lambda s, sl=sl: s.activation(
                    xn_t[sl][:], x_t[sl][:], AF.Identity,
                    bias=nm_t[sl][:, 0:1], scale=rs_t[sl][:, 0:1]),
                    deps=[t_nm, t_x] + rXN.war(sl))
                rXN.wrote(sl, t_xn)
                rX.read(sl, t_xn)
                rRS.read(sl, t_xn)
                rNM.read(sl, t_xn)
                for g in range(4):
                    ps = g % 2
                    tr_tok = None
                    war = rPTR.war(ps)
                    for j in range(8):
                        ec = g * 8 + j
                        tr_tok = op("pe", lambda t, sl=sl, ps=ps, j=j, ec=ec:
                                    t.transpose(ps_tr[ps][:, j * 128:(j + 1) * 128],
                                                xn_t[sl][:, ec * 128:(ec + 1) * 128],
                                                id_bf[:]),
                                    deps=[t_xn, t_id] + (war if j == 0 else []),
                                    inc=(j == 7))
                    rPTR.wrote(ps, tr_tok)
                    rXN.read(sl, tr_tok)
                    dst = slab3d[:, g * 8:(g + 1) * 8, ot * 128:(ot + 1) * 128]
                    src = ps_tr[ps][:].rearrange("p (a t) -> p a t", a=8)
                    t_ev = op("dve", lambda v, dst=dst, src=src: v.tensor_copy(
                        dst, src), deps=[tr_tok])
                    rPTR.read(ps, t_ev)
                    slab_tok.append(t_ev)
            bld.flush()

        q_sb = top.enter_context(nc.sbuf_tensor("q_sb", [128, H * 1024], BF16))
        q_tok = [None] * H

        with ExitStack() as sc1c:
            e1 = sc1c.enter_context
            wq_t = [e1(nc.sbuf_tensor(f"wq_t{i}", [128, NEC * 128], BF16)) for i in range(2)]
            ps_q = [e1(nc.psum_tensor(f"ps_q{i}", [128, 512], F32)) for i in range(2)]
            rWQ, rPQ = Ring(2), Ring(2)
            gate1c = bld.gate()
            for h in range(H):
                sl = h % 2
                src = wq[h].rearrange("(ec p) f -> p ec f", p=128)
                dst = wq_t[sl][:].rearrange("p (ec f) -> p ec f", ec=NEC)
                t_w = op("gp", lambda g, src=src, dst=dst: g.dma_start(
                    out=dst, in_=src, max_dma_last_dim=8192),
                    deps=rWQ.war(sl) + gate1c)
                rWQ.wrote(sl, t_w)
                for th in range(2):
                    mm_tok = None
                    war = rPQ.war(th)
                    for ec in range(NEC):
                        mm_tok = op("pe", lambda t, sl=sl, th=th, ec=ec: t.matmul(
                            ps_q[th][:],
                            wq_t[sl][:, ec * 128:(ec + 1) * 128],
                            xnt_slab[:, ec * 1024 + th * 512: ec * 1024 + th * 512 + 512],
                            start=(ec == 0), stop=(ec == NEC - 1)),
                            deps=[t_w] + slab_tok + (war if ec == 0 else []),
                            inc=(ec == NEC - 1))
                    rPQ.wrote(th, mm_tok)
                    rWQ.read(sl, mm_tok)
                    t_qe = op("dve", lambda v, h=h, th=th: v.tensor_copy(
                        q_sb[:, h * 1024 + th * 512: h * 1024 + th * 512 + 512],
                        ps_q[th][:]), deps=[mm_tok])
                    rPQ.read(th, t_qe)
                    q_tok[h] = t_qe
            bld.flush()

        if debug:
            g0 = bld.gate()
            op("sync", lambda s: s.dma_start(out=dbg_xnt, in_=xnt_slab[:]), deps=g0)
            op("sync", lambda s: s.dma_start(out=dbg_q, in_=q_sb[:]), deps=g0)
            op("sync", lambda s: s.dma_start(out=dbg_k, in_=k_fm[:]), deps=g0)
            op("sync", lambda s: s.dma_start(out=dbg_v, in_=v_tm[:]), deps=g0)
            bld.flush()

        attn_wr = [None] * H
        with ExitStack() as sc2:
            e2 = sc2.enter_context
            NP2 = 3
            NS2 = 4
            P_u = [e2(nc.sbuf_tensor(f"P_u{i}", [128, S], BF16)) for i in range(NP2)]
            vscr = [e2(nc.sbuf_tensor(f"vscr{i}", [128, 512], F32)) for i in range(NS2)]
            pt_sb = [e2(nc.sbuf_tensor(f"pt_sb{i}", [128, 1024], BF16)) for i in range(2)]
            sums = [e2(nc.sbuf_tensor(f"sums{i}", [128, 8], F32)) for i in range(NP2)]
            stot = [e2(nc.sbuf_tensor(f"stot{i}", [128, 1], F32)) for i in range(NP2)]
            rsum = [e2(nc.sbuf_tensor(f"rsum{i}", [128, 1], F32)) for i in range(NP2)]
            attn_t = [e2(nc.sbuf_tensor(f"attn_t{i}", [128, 128], BF16)) for i in range(NP2)]
            ps_s = [e2(nc.psum_tensor(f"ps_s{i}", [128, 512], F32)) for i in range(NS2)]
            ps_pt = [e2(nc.psum_tensor(f"ps_pt{i}", [128, 1024], BF16)) for i in range(2)]
            ps_av = [e2(nc.psum_tensor(f"ps_av{i}", [128, 128], F32)) for i in range(2)]

            rS, rV = Ring(NS2), Ring(NS2)
            rP, rSums, rStot, rRsum, rAT = (Ring(NP2) for _ in range(5))
            rPT, rPTS, rAV = (Ring(2) for _ in range(3))
            gsc = [0]

            gate2 = bld.gate()
            units = [(d, h) for d in range(1, NOWN + 1) for h in range(H)]

            def emit_A(ui):
                d, h = units[ui]
                u = ui % NP2
                exp_toks = []
                war_p = rP.war(u)
                war_sums = rSums.war(u)
                for kt in range(d):
                    m = d - 1 - kt
                    sl = gsc[0] % NS2
                    gsc[0] += 1
                    t_s = op("pe", lambda t, d=d, h=h, kt=kt, sl=sl: t.matmul(
                        ps_s[sl][:],
                        q_sb[:, h * 1024 + (d - 1) * 128: h * 1024 + d * 128],
                        k_fm[:, kt * 512:(kt + 1) * 512],
                        start=True, stop=True),
                        deps=[q_tok[h]] + [k_tok[i] for i in range(kt * 4, kt * 4 + 4)]
                             + rS.war(sl) + gate2)
                    rS.wrote(sl, t_s)
                    dmoff = 512 if m == 0 else 0
                    t_v = op("dve", lambda v, sl=sl, dmoff=dmoff, h=h: v.scalar_tensor_tensor(
                        vscr[sl][:], cst_dm[:, dmoff:dmoff + 512],
                        float(slopes[h]), ps_s[sl][:], ALU.mult, ALU.add),
                        deps=[t_s, t_dm] + rV.war(sl))
                    rV.wrote(sl, t_v)
                    rS.read(sl, t_v)
                    bcol = h * 8 + m
                    t_e = op("act", lambda s, u=u, kt=kt, sl=sl, bcol=bcol: s.activation(
                        P_u[u][:, kt * 512:(kt + 1) * 512], vscr[sl][:], AF.Exp,
                        bias=bias_sb[:, bcol:bcol + 1],
                        accum_out=sums[u][:, kt:kt + 1]),
                        deps=[t_v, t_bias] + (war_p + war_sums if kt == 0 else []))
                    rV.read(sl, t_e)
                    exp_toks.append(t_e)
                rP.wrote(u, exp_toks[-1])
                rSums.wrote(u, exp_toks[-1])
                return exp_toks

            def emit_B(ui, exp_toks):
                d, h = units[ui]
                u = ui % NP2
                u2 = ui % 2
                t_red = op("dve", lambda v, u=u, d=d: v.reduce_sum(
                    stot[u][:], sums[u][:, 0:d], mybir.AxisListType.X),
                    deps=exp_toks + rStot.war(u))
                rStot.wrote(u, t_red)
                rSums.read(u, t_red)
                t_red2 = op("dve", lambda v, u=u: v.tensor_scalar(
                    stot[u][:], stot[u][:], 1e-30, None, ALU.add), deps=[t_red])
                rStot.wrote(u, t_red2)
                t_rc = op("dve", lambda v, u=u: v.reciprocal(
                    rsum[u][:], stot[u][:]), deps=[t_red2] + rRsum.war(u))
                rRsum.wrote(u, t_rc)
                rStot.read(u, t_rc)
                t_nr = op("dve", lambda v, u=u, d=d: v.tensor_scalar(
                    P_u[u][:, 0:d * 512], P_u[u][:, 0:d * 512],
                    rsum[u][:, 0:1], None, ALU.mult), deps=[t_rc])
                rRsum.read(u, t_nr)
                rP.wrote(u, t_nr)
                nch = d * 4
                av_tok = None
                war_av = rAV.war(u2)
                for bank0 in range(0, nch, 8):
                    bp = (bank0 // 8) % 2
                    nb = min(8, nch - bank0)
                    tr_tok = None
                    war_pt = rPT.war(bp)
                    for j in range(nb):
                        ch = bank0 + j
                        tr_tok = op("pe", lambda t, bp=bp, j=j, u=u, ch=ch:
                                    t.transpose(ps_pt[bp][:, j * 128:(j + 1) * 128],
                                                P_u[u][:, ch * 128:(ch + 1) * 128],
                                                id_bf[:]),
                                    deps=[t_nr, t_id] + (war_pt if j == 0 else []),
                                    inc=(j == nb - 1))
                    rPT.wrote(bp, tr_tok)
                    rP.read(u, tr_tok)
                    t_pe = op("act", lambda s, bp=bp, nb=nb: s.copy(
                        pt_sb[bp][:, 0:nb * 128], ps_pt[bp][:, 0:nb * 128]),
                        deps=[tr_tok] + rPTS.war(bp))
                    rPTS.wrote(bp, t_pe)
                    rPT.read(bp, t_pe)
                    for j in range(nb):
                        ch = bank0 + j
                        av_tok = op("pe", lambda t, u2=u2, bp=bp, j=j, ch=ch, first=(ch == 0), last=(ch == nch - 1): t.matmul(
                            ps_av[u2][:],
                            v_tm[:, ch * 128:(ch + 1) * 128],
                            pt_sb[bp][:, j * 128:(j + 1) * 128],
                            start=first, stop=last),
                            deps=[t_pe, kv_tok[ch]] + (war_av if ch == 0 else []),
                            inc=(j == nb - 1))
                    rPTS.read(bp, av_tok)
                rAV.wrote(u2, av_tok)
                t_ae = op("dve", lambda v, u=u, u2=u2: v.tensor_copy(
                    attn_t[u][:], ps_av[u2][:]), deps=[av_tok] + rAT.war(u))
                rAT.wrote(u, t_ae)
                rAV.read(u2, t_ae)
                t_aw = op("sync", lambda s, u=u, h=h, d=d: s.dma_start(
                    out=attn_d[h, :, (d - 1) * 128:d * 128], in_=attn_t[u][:]),
                    deps=[t_ae])
                rAT.read(u, t_aw)
                attn_wr[h] = t_aw

            LAG = 2
            prevs = []
            for ui in range(len(units)):
                toks = emit_A(ui)
                prevs.append((ui, toks))
                if len(prevs) > LAG:
                    emit_B(*prevs.pop(0))
            for p in prevs:
                emit_B(*p)
            bld.flush()

        s_wr = {}
        with ExitStack() as scw:
            e2 = scw.enter_context
            wc_t = [e2(nc.sbuf_tensor(f"wc_t{i}", [128, 512], BF16)) for i in range(4)]
            at_t = [e2(nc.sbuf_tensor(f"at_t{i}", [128, 1024], BF16)) for i in range(4)]
            s_t = [e2(nc.sbuf_tensor(f"s_t{i}", [128, 512], F32)) for i in range(2)]
            ps_wc = [e2(nc.psum_tensor(f"ps_wc{i}", [128, 512], F32)) for i in range(8)]
            rWC, rAT2, rST = Ring(4), Ring(4), Ring(2)
            rPW = Ring(8)
            gatew = bld.gate()
            for ep in range(NEP):
                stop_toks = [None] * 8
                for hc in range(H):
                    sl = hc % 4
                    t_w = op("gp", lambda g, hc=hc, ep=ep, sl=sl: g.dma_start(
                        out=wc_t[sl][:],
                        in_=wc[hc * 128:(hc + 1) * 128, ep * 512:(ep + 1) * 512]),
                        deps=rWC.war(sl) + gatew)
                    rWC.wrote(sl, t_w)
                    t_a = op("sync", lambda s, hc=hc, sl=sl: s.dma_start(
                        out=at_t[sl][:], in_=attn_d[hc]),
                        deps=[attn_wr[hc]] + rAT2.war(sl) + gatew)
                    rAT2.wrote(sl, t_a)
                    for dd in range(8):
                        war = rPW.war(dd) if hc == 0 else []
                        tok = op("pe", lambda t, sl=sl, dd=dd, hc=hc: t.matmul(
                            ps_wc[dd][:],
                            at_t[sl][:, dd * 128:(dd + 1) * 128],
                            wc_t[sl][:],
                            start=(hc == 0), stop=(hc == H - 1)),
                            deps=[t_w, t_a] + war,
                            inc=(hc == H - 1 or dd == 7))
                        if dd == 7:
                            rWC.read(sl, tok)
                            rAT2.read(sl, tok)
                        if hc == H - 1:
                            stop_toks[dd] = tok
                            rPW.wrote(dd, tok)
                for dd in range(8):
                    sl = dd % 2
                    t_se = op("act", lambda s, dd=dd, sl=sl: s.copy(
                        s_t[sl][:], ps_wc[dd][:]),
                        deps=[stop_toks[dd]] + rST.war(sl))
                    rST.wrote(sl, t_se)
                    rPW.read(dd, t_se)
                    t_sw = op("sync", lambda s, dd=dd, ep=ep, sl=sl: s.dma_start(
                        out=s_d[dd * 128:(dd + 1) * 128, ep * 512:(ep + 1) * 512],
                        in_=s_t[sl][:]), deps=[t_se])
                    rST.read(sl, t_sw)
                    s_wr[(dd, ep)] = t_sw
            bld.flush()

        h_wr = [None] * NFC
        with ExitStack() as s3a:
            e3 = s3a.enter_context
            wg_t = [e3(nc.sbuf_tensor(f"wg_t{i}", [128, NEC * 128], BF16)) for i in range(2)]
            wu_t = [e3(nc.sbuf_tensor(f"wu_t{i}", [128, NEC * 128], BF16)) for i in range(2)]
            sg_t = [e3(nc.sbuf_tensor(f"sg_t{i}", [128, 512], F32)) for i in range(2)]
            h_t = [e3(nc.sbuf_tensor(f"h_t{i}", [128, 1024], BF16)) for i in range(2)]
            ps_f = [e3(nc.psum_tensor(f"ps_f{i}", [128, 512], F32)) for i in range(8)]
            rWG, rWU, rSG, rHT = Ring(2), Ring(2), Ring(2), Ring(2)
            rPF = Ring(8)
            gate3 = bld.gate()
            for fc in range(NFC):
                sl = fc % 2
                srcg = wg[fc]
                dstg = wg_t[sl][:].rearrange("p (ec f) -> p ec f", ec=NEC)
                t_wg = op("gp", lambda g, srcg=srcg, dstg=dstg: g.dma_start(
                    out=dstg, in_=srcg, max_dma_last_dim=8192),
                    deps=rWG.war(sl) + gate3)
                rWG.wrote(sl, t_wg)
                srcu = wu[fc]
                dstu = wu_t[sl][:].rearrange("p (ec f) -> p ec f", ec=NEC)
                t_wu = op("gp", lambda g, srcu=srcu, dstu=dstu: g.dma_start(
                    out=dstu, in_=srcu, max_dma_last_dim=8192),
                    deps=rWU.war(sl) + gate3)
                rWU.wrote(sl, t_wu)
                hm_toks = []
                for th in range(2):
                    bg = sl * 4 + th * 2
                    bu = bg + 1
                    war_g = rPF.war(bg)
                    war_u = rPF.war(bu)
                    g_tok = None
                    u_tok = None
                    for ec in range(NEC):
                        g_tok = op("pe", lambda t, sl=sl, th=th, ec=ec, bg=bg: t.matmul(
                            ps_f[bg][:],
                            wg_t[sl][:, ec * 128:(ec + 1) * 128],
                            xnt_slab[:, ec * 1024 + th * 512: ec * 1024 + th * 512 + 512],
                            start=(ec == 0), stop=(ec == NEC - 1)),
                            deps=[t_wg] + slab_tok + (war_g if ec == 0 else []),
                            inc=(ec == NEC - 1))
                    for ec in range(NEC):
                        u_tok = op("pe", lambda t, sl=sl, th=th, ec=ec, bu=bu: t.matmul(
                            ps_f[bu][:],
                            wu_t[sl][:, ec * 128:(ec + 1) * 128],
                            xnt_slab[:, ec * 1024 + th * 512: ec * 1024 + th * 512 + 512],
                            start=(ec == 0), stop=(ec == NEC - 1)),
                            deps=[t_wu] + slab_tok + (war_u if ec == 0 else []),
                            inc=(ec == NEC - 1))
                    rPF.wrote(bg, g_tok)
                    rPF.wrote(bu, u_tok)
                    rWG.read(sl, g_tok)
                    rWU.read(sl, u_tok)
                    ssl = th
                    t_sg = op("act", lambda s, bg=bg, ssl=ssl: s.activation(
                        sg_t[ssl][:], ps_f[bg][:], AF.Silu),
                        deps=[g_tok] + rSG.war(ssl))
                    rSG.wrote(ssl, t_sg)
                    rPF.read(bg, t_sg)
                    war_h = rHT.war(sl) if th == 0 else []
                    t_hm = op("dve", lambda v, sl=sl, th=th, ssl=ssl, bu=bu: v.tensor_tensor(
                        h_t[sl][:, th * 512:(th + 1) * 512], sg_t[ssl][:],
                        ps_f[bu][:], ALU.mult), deps=[t_sg, u_tok] + war_h)
                    rSG.read(ssl, t_hm)
                    rPF.read(bu, t_hm)
                    hm_toks.append(t_hm)
                rHT.wrote(sl, hm_toks[-1])
                t_hw = op("sync", lambda s, fc=fc, sl=sl: s.dma_start(
                    out=h_d[fc], in_=h_t[sl][:]), deps=hm_toks)
                rHT.read(sl, t_hw)
                h_wr[fc] = t_hw
            bld.flush()

        with ExitStack() as s3b:
            e3 = s3b.enter_context
            h2_t = [e3(nc.sbuf_tensor(f"h2_t{i}", [128, 1024], BF16)) for i in range(4)]
            wd_t = [e3(nc.sbuf_tensor(f"wd_t{i}", [128, 512], BF16)) for i in range(4)]
            s_sl = [e3(nc.sbuf_tensor(f"s_sl{i}", [128, 512], F32)) for i in range(8)]
            x_sl = [e3(nc.sbuf_tensor(f"x_sl{i}", [128, 512], F32)) for i in range(8)]
            g_t = [e3(nc.sbuf_tensor(f"g_t{i}", [128, 512], F32)) for i in range(2)]
            o_t = [e3(nc.sbuf_tensor(f"o_t{i}", [128, 512], F32)) for i in range(2)]
            o2_t = [e3(nc.sbuf_tensor(f"o2_t{i}", [128, 512], F32)) for i in range(2)]
            ps_y = [e3(nc.psum_tensor(f"ps_y{i}", [128, 512], F32)) for i in range(8)]
            rH2, rWD, rG, rO, rO2 = Ring(4), Ring(4), Ring(2), Ring(2), Ring(2)
            rSL, rXL = Ring(8), Ring(8)
            rPY = Ring(8)
            gate3b = bld.gate()
            for ep in range(NEP):
                sx_toks = []
                for tt in range(8):
                    t_sx = op("gp", lambda g, tt=tt, ep=ep: g.dma_start(
                        out=s_sl[tt][:],
                        in_=s_d[tt * 128:(tt + 1) * 128, ep * 512:(ep + 1) * 512]),
                        deps=[s_wr[(tt, ep)]] + rSL.war(tt) + gate3b)
                    rSL.wrote(tt, t_sx)
                    t_xx = op("gp", lambda g, tt=tt, ep=ep: g.dma_start(
                        out=x_sl[tt][:],
                        in_=xown[tt][:, ep * 512:(ep + 1) * 512]),
                        deps=rXL.war(tt) + gate3b)
                    rXL.wrote(tt, t_xx)
                    sx_toks.append((t_sx, t_xx))
                stop_toks = [None] * 8
                for fc in range(NFC):
                    sl = fc % 4
                    t_h = op("sync", lambda s, fc=fc, sl=sl: s.dma_start(
                        out=h2_t[sl][:], in_=h_d[fc]),
                        deps=[h_wr[fc]] + rH2.war(sl) + gate3b)
                    rH2.wrote(sl, t_h)
                    t_w = op("gp", lambda g, ep=ep, fc=fc, sl=sl: g.dma_start(
                        out=wd_t[sl][:], in_=wd[ep, fc]),
                        deps=rWD.war(sl) + gate3b)
                    rWD.wrote(sl, t_w)
                    for tt in range(8):
                        war = rPY.war(tt) if fc == 0 else []
                        tok = op("pe", lambda t, sl=sl, tt=tt, fc=fc: t.matmul(
                            ps_y[tt][:],
                            h2_t[sl][:, tt * 128:(tt + 1) * 128],
                            wd_t[sl][:],
                            start=(fc == 0), stop=(fc == NFC - 1)),
                            deps=[t_h, t_w] + war,
                            inc=(fc == NFC - 1 or tt == 7))
                        if tt == 7:
                            rH2.read(sl, tok)
                            rWD.read(sl, tok)
                        if fc == NFC - 1:
                            stop_toks[tt] = tok
                            rPY.wrote(tt, tok)
                for tt in range(8):
                    sl = tt % 2
                    t_g = op("act", lambda s, tt=tt, sl=sl: s.activation(
                        g_t[sl][:], ps_y[tt][:], AF.Silu),
                        deps=[stop_toks[tt]] + rG.war(sl))
                    rG.wrote(sl, t_g)
                    rPY.read(tt, t_g)
                    t_m = op("dve", lambda v, tt=tt, sl=sl: v.tensor_mul(
                        o_t[sl][:], s_sl[tt][:], g_t[sl][:]),
                        deps=[t_g, sx_toks[tt][0]] + rO.war(sl))
                    rO.wrote(sl, t_m)
                    rG.read(sl, t_m)
                    rSL.read(tt, t_m)
                    t_a = op("dve", lambda v, tt=tt, sl=sl: v.tensor_add(
                        o2_t[sl][:], o_t[sl][:], x_sl[tt][:]),
                        deps=[t_m, sx_toks[tt][1]] + rO2.war(sl))
                    rO2.wrote(sl, t_a)
                    rO.read(sl, t_a)
                    rXL.read(tt, t_a)
                    t_ow = op("sync", lambda s, tt=tt, ep=ep, sl=sl: s.dma_start(
                        out=out[tt][:, ep * 512:(ep + 1) * 512], in_=o2_t[sl][:]),
                        deps=[t_a])
                    rO2.read(sl, t_ow)
            final = bld.count["sync"]
            fsem = bld.sems["sync"]
            bld.raw("sync", lambda s: s.wait_ge(fsem, final))
            bld.flush()

    return nc


def _prep_shared(w_qkv, w_c, W_gate, W_up, W_down):
    wkv = np.ascontiguousarray(w_qkv[:, E:E + 256])
    wqp = np.ascontiguousarray(
        w_qkv[:, :E].reshape(E, H, 128).transpose(1, 0, 2))
    wgp = np.ascontiguousarray(
        W_gate.reshape(NFC, 128, NEC, 128).transpose(0, 3, 2, 1))
    wup = np.ascontiguousarray(
        W_up.reshape(NFC, 128, NEC, 128).transpose(0, 3, 2, 1))
    wdp = np.ascontiguousarray(
        W_down.reshape(NEP, 512, NFC, 128).transpose(0, 2, 3, 1))
    return wkv, wqp, np.ascontiguousarray(w_c), wgp, wup, wdp


def _consts_for_slot(slot):
    slopes = _alibi_slopes()
    j = np.arange(512, dtype=np.float64)[None, :]
    i = np.arange(128, dtype=np.float64)[:, None]
    gen = (j - i).astype(np.float32)
    diag = gen.copy()
    diag[(j - i) > 128 * slot] = MASK_NEG
    dm = np.concatenate([gen, diag], axis=1).astype(np.float32)
    bias = np.zeros((128, H * 8), dtype=np.float32)
    for h in range(H):
        for m in range(8):
            delta = -512.0 * m - 128.0 * slot
            bias[:, h * 8 + m] = np.float32(-BSTAT + float(slopes[h]) * delta)
    ident = np.eye(128, dtype=np.float32)
    msc = np.zeros((128, 8), dtype=np.float32)
    msc[:, 0] = 1e-5
    return dm, bias, ident, msc


def kernel(L0_attn_hidden_states, L0_attn_w_qkv, L0_attn_w_c,
           L0_ffn_W_gate, L0_ffn_W_up, L0_ffn_W_down, _debug=False):
    x = np.asarray(L0_attn_hidden_states, dtype=np.float32)
    wkv, wqp, wcc, wgp, wup, wdp = _prep_shared(
        np.asarray(L0_attn_w_qkv, dtype=np.float32),
        np.asarray(L0_attn_w_c, dtype=np.float32),
        np.asarray(L0_ffn_W_gate, dtype=np.float32),
        np.asarray(L0_ffn_W_up, dtype=np.float32),
        np.asarray(L0_ffn_W_down, dtype=np.float32))

    nc = build_nc(debug=_debug)
    in_maps = []
    for core in range(8):
        b, slot = core // 4, core % 4
        dm, bias, ident, msc = _consts_for_slot(slot)
        xb = np.ascontiguousarray(x[b])
        xown = np.ascontiguousarray(
            xb.reshape(8, 4, 128, E)[:, slot])
        in_maps.append({
            "xkv": xb, "xown": xown, "wkv": wkv, "wq": wqp, "wc": wcc,
            "wg": wgp, "wu": wup, "wd": wdp,
            "dmc": dm, "biasc": bias, "identc": ident, "mscc": msc,
        })
    res = run_bass_kernel_spmd(nc, in_maps, core_ids=list(range(8)))
    full = np.empty((B, S, E), dtype=np.float32)
    for core in range(8):
        b, slot = core // 4, core % 4
        o = np.asarray(res.results[core]["out"])
        full[b].reshape(8, 4, 128, E)[:, slot] = o
    if _debug:
        return full, res
    return full


# version 03 (final)
# speedup vs baseline: 1.2247x; 1.0645x over previous
import math
from contextlib import ExitStack

import numpy as np
import ml_dtypes
import concourse.bass as bass
import concourse.mybir as mybir
from concourse.bass_utils import run_bass_kernel_spmd

F32 = mybir.dt.float32
BF16 = mybir.dt.bfloat16
AF = mybir.ActivationFunctionType
ALU = mybir.AluOpType

B, S, E, H, DH, FF = 2, 4096, 4096, 32, 128, 11008
NEC = E // 128
NFC = FF // 128
NTT = S // 128
NOWN = 8
NEP = E // 512
SCALING = float(128 ** -0.5)
BSTAT = 30.0
MASK_NEG = -1.0e36


def _alibi_slopes():
    base = 2.0 ** (-(2.0 ** -(math.log2(32) - 3)))
    return (base ** np.arange(1, 33, dtype=np.float64)).astype(np.float32)


class _Op:
    __slots__ = ("q", "waits", "fn", "inc")

    def __init__(self, q, waits, fn, inc):
        self.q, self.waits, self.fn, self.inc = q, waits, fn, inc


class Builder:
    QUEUES = {"pe": 1, "dve": 1, "act": 1, "sync": 16, "gp": 16}
    ENG = {"pe": "tensor", "dve": "vector", "act": "scalar",
           "sync": "sync", "gp": "gpsimd"}

    def __init__(self, nc, stack):
        self.nc = nc
        self.sems = {q: stack.enter_context(nc.semaphore(f"ctr_{q}"))
                     for q in self.QUEUES}
        self.count = {q: 0 for q in self.QUEUES}
        self.hw = {q: {} for q in self.QUEUES}
        self.pending = []

    def op(self, q, fn, deps=(), inc=True):
        merged = {}
        for tok in deps:
            if tok is None:
                continue
            dq, dv = tok
            if dq == q and q == "pe":
                continue
            if merged.get(dq, 0) < dv:
                merged[dq] = dv
        hw = self.hw[q]
        waits = []
        for dq, dv in merged.items():
            if hw.get(dq, 0) >= dv:
                continue
            waits.append((dq, self.sems[dq], dv))
            hw[dq] = dv
        qty = self.QUEUES[q] if inc else 0
        tok = None
        if inc:
            self.count[q] += qty
            tok = (q, self.count[q])
        self.pending.append(_Op(q, waits, fn, qty))
        return tok

    def gate(self):
        return [(q, n) for q, n in self.count.items() if n > 0]

    def eset(self, tok):
        if tok is None:
            return None
        q, v = tok
        if q in ("dve", "act"):
            return (q, v + 1)
        return tok

    def dset(self, tok):
        if tok is None:
            return None
        q, v = tok
        if q in ("sync", "gp"):
            return (q, v + 16)
        return tok

    def raw(self, q, fn):
        self.pending.append(_Op(q, [], fn, 0))

    def flush(self):
        ops = self.pending
        self.pending = []
        if not ops:
            return
        with self.nc.Block() as blk:
            for q, eng_attr in self.ENG.items():
                qops = [o for o in ops if o.q == q]
                if not qops:
                    continue

                def emit(eng, qops=qops, q=q):
                    for o in qops:
                        for wq, sem, val in o.waits:
                            eng.wait_ge(sem, min(val, self.count[wq]))
                        inst = o.fn(eng)
                        if o.inc:
                            inst.then_inc(self.sems[q], o.inc)

                getattr(blk, eng_attr)(emit)


class Ring:

    def __init__(self, n):
        self.n = n
        self.wr = [None] * n
        self.rd = [[] for _ in range(n)]

    def war(self, s):
        deps = list(self.rd[s])
        self.rd[s] = []
        if self.wr[s] is not None:
            deps.append(self.wr[s])
        return deps

    def wrote(self, s, tok):
        self.wr[s] = tok

    def read(self, s, tok):
        if tok is not None:
            self.rd[s].append(tok)


def build_nc(debug=False):
    nc = bass.Bass("TRN2", target_bir_lowering=False)

    xkv = nc.dram_tensor("xkv", [S, E], F32, kind="ExternalInput").ap()
    xown = nc.dram_tensor("xown", [NOWN, 128, E], F32, kind="ExternalInput").ap()
    wkv = nc.dram_tensor("wkv", [E, 256], F32, kind="ExternalInput").ap()
    wq = nc.dram_tensor("wq", [H, E, 128], F32, kind="ExternalInput").ap()
    wc = nc.dram_tensor("wc", [E, E], F32, kind="ExternalInput").ap()
    wg = nc.dram_tensor("wg", [NFC, 128, NEC, 128], F32, kind="ExternalInput").ap()
    wu = nc.dram_tensor("wu", [NFC, 128, NEC, 128], F32, kind="ExternalInput").ap()
    wd = nc.dram_tensor("wd", [NEP, NFC, 128, 512], F32, kind="ExternalInput").ap()
    dmc = nc.dram_tensor("dmc", [128, 1024], F32, kind="ExternalInput").ap()
    biasc = nc.dram_tensor("biasc", [128, H * 8], F32, kind="ExternalInput").ap()
    identc = nc.dram_tensor("identc", [128, 128], F32, kind="ExternalInput").ap()
    mscc = nc.dram_tensor("mscc", [128, 8], F32, kind="ExternalInput").ap()
    out = nc.dram_tensor("out", [NOWN, 128, E], F32, kind="ExternalOutput").ap()

    skind = "ExternalOutput" if debug else "Internal"
    attn_d = nc.dram_tensor("attn_d", [H, 128, 1024], BF16, kind=skind).ap()
    s_d = nc.dram_tensor("s_d", [1024, E], F32, kind=skind).ap()
    h_d = nc.dram_tensor("h_d", [NFC, 128, 1024], BF16, kind=skind).ap()
    if debug:
        dbg_g = nc.dram_tensor("dbg_g", [NEP, 8, 128, 512], F32, kind="ExternalOutput").ap()
        dbg_o = nc.dram_tensor("dbg_o", [NEP, 8, 128, 512], F32, kind="ExternalOutput").ap()
        dbg_xnt = nc.dram_tensor("dbg_xnt", [128, NEC * 1024], BF16, kind="ExternalOutput").ap()
        dbg_q = nc.dram_tensor("dbg_q", [128, H * 1024], BF16, kind="ExternalOutput").ap()
        dbg_k = nc.dram_tensor("dbg_k", [128, S], BF16, kind="ExternalOutput").ap()
        dbg_v = nc.dram_tensor("dbg_v", [128, NTT * 128], BF16, kind="ExternalOutput").ap()

    slopes = _alibi_slopes()

    with ExitStack() as top:
        bld = Builder(nc, top)
        op = bld.op

        e = top.enter_context
        cst_dm = e(nc.sbuf_tensor("cst_dm", [128, 1024], F32))
        bias_sb = e(nc.sbuf_tensor("bias_sb", [128, H * 8], F32))
        msc_sb = e(nc.sbuf_tensor("msc_sb", [128, 8], F32))
        id_bf = e(nc.sbuf_tensor("id_bf", [128, 128], BF16))
        k_fm = e(nc.sbuf_tensor("k_fm", [128, S], BF16))
        v_tm = e(nc.sbuf_tensor("v_tm", [128, NTT * 128], BF16))

        junk_sb = e(nc.sbuf_tensor("junk_sb", [128, 8], F32))

        def fence(q):
            return op(q, lambda en: en.dma_start(out=junk_sb[:], in_=mscc))

        def efence():
            op("dve", lambda v: v.tensor_copy(junk_sb[:, 0:1], junk_sb[:, 1:2]))
            op("act", lambda sc: sc.copy(junk_sb[:, 2:3], junk_sb[:, 3:4]))

        t_dm = op("sync", lambda s: s.dma_start(out=cst_dm[:], in_=dmc))
        t_bias = op("sync", lambda s: s.dma_start(out=bias_sb[:], in_=biasc))
        t_msc = op("sync", lambda s: s.dma_start(out=msc_sb[:], in_=mscc))
        t_id = op("gp", lambda g: g.dma_start(out=id_bf[:], in_=identc))

        kv_tok = [None] * NTT
        k_tok = [None] * NTT

        with ExitStack() as sc1:
            e1 = sc1.enter_context
            x_t = [e1(nc.sbuf_tensor(f"x_t{i}", [128, E], F32)) for i in range(2)]
            xn_t = [e1(nc.sbuf_tensor(f"xn_t{i}", [128, E], BF16)) for i in range(2)]
            xnt_t = [e1(nc.sbuf_tensor(f"xnt_t{i}", [128, NEC * 128], BF16)) for i in range(2)]
            wkv_sb = e1(nc.sbuf_tensor("wkv_sb", [128, NEC * 256], BF16))
            bn_t = [e1(nc.sbuf_tensor(f"bn_t{i}", [128, 48], F32)) for i in range(2)]
            mv_t = [e1(nc.sbuf_tensor(f"mv_t{i}", [128, 2], F32)) for i in range(2)]
            sd_t = [e1(nc.sbuf_tensor(f"sd_t{i}", [128, 1], F32)) for i in range(2)]
            rs_t = [e1(nc.sbuf_tensor(f"rs_t{i}", [128, 1], F32)) for i in range(2)]
            nm_t = [e1(nc.sbuf_tensor(f"nm_t{i}", [128, 1], F32)) for i in range(2)]
            ktm_t = [e1(nc.sbuf_tensor(f"ktm_t{i}", [128, 128], BF16)) for i in range(2)]
            ps_tr = [e1(nc.psum_tensor(f"ps_tr{i}", [128, 1024], BF16)) for i in range(2)]
            ps_kv = [e1(nc.psum_tensor(f"ps_kv{i}", [128, 256], F32)) for i in range(2)]
            ps_kt = [e1(nc.psum_tensor(f"ps_kt{i}", [128, 128], BF16)) for i in range(2)]

            wkv_r = wkv.rearrange("(ec p) c -> p ec c", p=128)
            wkv_dst = wkv_sb[:].rearrange("p (ec c) -> p ec c", ec=NEC)
            t_wkv = op("gp", lambda g: g.dma_start(out=wkv_dst, in_=wkv_r,
                                                   max_dma_last_dim=8192))

            rX, rXN, rXNT = Ring(2), Ring(2), Ring(2)
            rBN, rMV, rSD, rRS, rNM = Ring(2), Ring(2), Ring(2), Ring(2), Ring(2)
            rKTM, rPTR, rPKV, rPKT = Ring(2), Ring(2), Ring(2), Ring(2)

            def ln_pipeline(src_ap_fn, n_tiles, xnt_dest_fn, kv_consume):
                for tt in range(n_tiles):
                    sl = tt % 2
                    t_x = op("gp", lambda g, tt=tt, sl=sl: g.dma_start(
                        out=x_t[sl][:], in_=src_ap_fn(tt)), deps=rX.war(sl))
                    rX.wrote(sl, t_x)
                    stat_toks = []
                    d_x = bld.dset(t_x)
                    for i in range(8):
                        t = op("dve", lambda v, sl=sl, i=i: v.bn_stats(
                            bn_t[sl][:, i * 6:(i + 1) * 6],
                            x_t[sl][:, i * 512:(i + 1) * 512]),
                            deps=[d_x] + (rBN.war(sl) if i == 0 else []))
                        stat_toks.append(t)
                        rX.read(sl, t)
                    rBN.wrote(sl, stat_toks[-1])
                    t_mv = op("dve", lambda v, sl=sl: v.bn_aggr(
                        mv_t[sl][:], bn_t[sl][:]), deps=stat_toks + rMV.war(sl))
                    rMV.wrote(sl, t_mv)
                    rBN.read(sl, t_mv)
                    t_sd = op("act", lambda s, sl=sl: s.activation(
                        sd_t[sl][:], mv_t[sl][:, 1:2], AF.Sqrt,
                        bias=msc_sb[:, 0:1]), deps=[t_mv, t_msc] + rSD.war(sl))
                    rSD.wrote(sl, t_sd)
                    t_rs = op("dve", lambda v, sl=sl: v.reciprocal(
                        rs_t[sl][:], sd_t[sl][:]), deps=[t_sd] + rRS.war(sl))
                    rRS.wrote(sl, t_rs)
                    rSD.read(sl, t_rs)
                    t_nm = op("dve", lambda v, sl=sl: v.tensor_scalar(
                        nm_t[sl][:], mv_t[sl][:, 0:1], rs_t[sl][:, 0:1], -1.0,
                        ALU.mult, ALU.mult), deps=[t_rs] + rNM.war(sl))
                    rNM.wrote(sl, t_nm)
                    rMV.read(sl, t_nm)
                    t_xn = op("act", lambda s, sl=sl: s.activation(
                        xn_t[sl][:], x_t[sl][:], AF.Identity,
                        bias=nm_t[sl][:, 0:1], scale=rs_t[sl][:, 0:1]),
                        deps=[t_nm, d_x] + rXN.war(sl))
                    rXN.wrote(sl, t_xn)
                    rX.read(sl, t_xn)
                    rRS.read(sl, t_xn)
                    rNM.read(sl, t_xn)
                    ev_toks = []
                    for g in range(4):
                        ps = g % 2
                        tr_tok = None
                        war = rPTR.war(ps)
                        for j in range(8):
                            ec = g * 8 + j
                            tr_tok = op("pe", lambda t, sl=sl, ps=ps, j=j, ec=ec:
                                        t.transpose(ps_tr[ps][:, j * 128:(j + 1) * 128],
                                                    xn_t[sl][:, ec * 128:(ec + 1) * 128],
                                                    id_bf[:]),
                                        deps=[t_xn, t_id] + (war if j == 0 else []),
                                        inc=(j == 7))
                        rPTR.wrote(ps, tr_tok)
                        rXN.read(sl, tr_tok)
                        dst, dslot = xnt_dest_fn(tt, g)
                        dep_war = rXNT.war(dslot) if dslot is not None else []
                        t_ev = op("dve", lambda v, dst=dst, ps=ps: v.tensor_copy(
                            dst, ps_tr[ps][:]), deps=[tr_tok] + dep_war)
                        if dslot is not None:
                            rXNT.wrote(dslot, t_ev)
                        rPTR.read(ps, t_ev)
                        ev_toks.append(t_ev)
                    if not kv_consume:
                        continue
                    mm_tok = None
                    war = rPKV.war(sl)
                    for ec in range(NEC):
                        mm_tok = op("pe", lambda t, sl=sl, ec=ec: t.matmul(
                            ps_kv[sl][:],
                            xnt_t[sl][:, ec * 128:(ec + 1) * 128],
                            wkv_sb[:, ec * 256:(ec + 1) * 256],
                            start=(ec == 0), stop=(ec == NEC - 1)),
                            deps=[ev_toks[ec // 8], bld.dset(t_wkv)] + (war if ec == 0 else []),
                            inc=(ec == NEC - 1))
                    rPKV.wrote(sl, mm_tok)
                    for t_ev in ev_toks:
                        rXNT.read(sl, mm_tok)
                    t_ks = op("dve", lambda v, sl=sl: v.tensor_scalar(
                        ktm_t[sl][:], ps_kv[sl][:, 0:128], SCALING, None,
                        ALU.mult), deps=[mm_tok] + rKTM.war(sl))
                    rKTM.wrote(sl, t_ks)
                    rPKV.read(sl, t_ks)
                    t_vc = op("dve", lambda v, sl=sl, tt=tt: v.tensor_copy(
                        v_tm[:, tt * 128:(tt + 1) * 128], ps_kv[sl][:, 128:256]),
                        deps=[mm_tok])
                    rPKV.read(sl, t_vc)
                    kv_tok[tt] = t_vc
                    t_kt = op("pe", lambda t, sl=sl: t.transpose(
                        ps_kt[sl][:], ktm_t[sl][:], id_bf[:]),
                        deps=[t_ks, t_id] + rPKT.war(sl))
                    rPKT.wrote(sl, t_kt)
                    rKTM.read(sl, t_kt)
                    t_ke = op("dve", lambda v, sl=sl, tt=tt: v.tensor_copy(
                        k_fm[:, tt * 128:(tt + 1) * 128], ps_kt[sl][:]),
                        deps=[t_kt])
                    rPKT.read(sl, t_ke)
                    k_tok[tt] = t_ke

            def xnt_dest_1a(tt, g):
                sl = tt % 2
                return xnt_t[sl][:, g * 1024:(g + 1) * 1024], sl

            ln_pipeline(lambda tt: xkv[tt * 128:(tt + 1) * 128, :], NTT,
                        xnt_dest_1a, True)
            fence("gp")
            efence()
            bld.flush()

        gate1b = bld.gate()
        xnt_slab = top.enter_context(
            nc.sbuf_tensor("xnt_slab", [128, NEC * 1024], BF16))
        slab3d = xnt_slab[:].rearrange("p (ec t) -> p ec t", ec=NEC)
        slab_tok = []

        with ExitStack() as sc1b:
            e1 = sc1b.enter_context
            x_t = [e1(nc.sbuf_tensor(f"xo_t{i}", [128, E], F32)) for i in range(2)]
            xn_t = [e1(nc.sbuf_tensor(f"xno_t{i}", [128, E], BF16)) for i in range(2)]
            bn_t = [e1(nc.sbuf_tensor(f"bno_t{i}", [128, 48], F32)) for i in range(2)]
            mv_t = [e1(nc.sbuf_tensor(f"mvo_t{i}", [128, 2], F32)) for i in range(2)]
            sd_t = [e1(nc.sbuf_tensor(f"sdo_t{i}", [128, 1], F32)) for i in range(2)]
            rs_t = [e1(nc.sbuf_tensor(f"rso_t{i}", [128, 1], F32)) for i in range(2)]
            nm_t = [e1(nc.sbuf_tensor(f"nmo_t{i}", [128, 1], F32)) for i in range(2)]
            ps_tr = [e1(nc.psum_tensor(f"ps_tro{i}", [128, 1024], BF16)) for i in range(2)]

            rX, rXN = Ring(2), Ring(2)
            rBN, rMV, rSD, rRS, rNM, rPTR = (Ring(2) for _ in range(6))

            for ot in range(NOWN):
                sl = ot % 2
                t_x = op("gp", lambda g, ot=ot, sl=sl: g.dma_start(
                    out=x_t[sl][:], in_=xown[ot]), deps=rX.war(sl) + gate1b)
                rX.wrote(sl, t_x)
                stat_toks = []
                d_x = bld.dset(t_x)
                for i in range(8):
                    t = op("dve", lambda v, sl=sl, i=i: v.bn_stats(
                        bn_t[sl][:, i * 6:(i + 1) * 6],
                        x_t[sl][:, i * 512:(i + 1) * 512]),
                        deps=[d_x] + (rBN.war(sl) if i == 0 else []) + gate1b)
                    stat_toks.append(t)
                    rX.read(sl, t)
                rBN.wrote(sl, stat_toks[-1])
                t_mv = op("dve", lambda v, sl=sl: v.bn_aggr(
                    mv_t[sl][:], bn_t[sl][:]), deps=stat_toks + rMV.war(sl))
                rMV.wrote(sl, t_mv)
                rBN.read(sl, t_mv)
                t_sd = op("act", lambda s, sl=sl: s.activation(
                    sd_t[sl][:], mv_t[sl][:, 1:2], AF.Sqrt,
                    bias=msc_sb[:, 0:1]), deps=[t_mv] + rSD.war(sl) + gate1b)
                rSD.wrote(sl, t_sd)
                t_rs = op("dve", lambda v, sl=sl: v.reciprocal(
                    rs_t[sl][:], sd_t[sl][:]), deps=[t_sd] + rRS.war(sl))
                rRS.wrote(sl, t_rs)
                rSD.read(sl, t_rs)
                t_nm = op("dve", lambda v, sl=sl: v.tensor_scalar(
                    nm_t[sl][:], mv_t[sl][:, 0:1], rs_t[sl][:, 0:1], -1.0,
                    ALU.mult, ALU.mult), deps=[t_rs] + rNM.war(sl))
                rNM.wrote(sl, t_nm)
                rMV.read(sl, t_nm)
                t_xn = op("act", lambda s, sl=sl: s.activation(
                    xn_t[sl][:], x_t[sl][:], AF.Identity,
                    bias=nm_t[sl][:, 0:1], scale=rs_t[sl][:, 0:1]),
                    deps=[t_nm, d_x] + rXN.war(sl))
                rXN.wrote(sl, t_xn)
                rX.read(sl, t_xn)
                rRS.read(sl, t_xn)
                rNM.read(sl, t_xn)
                for g in range(4):
                    ps = g % 2
                    tr_tok = None
                    war = rPTR.war(ps)
                    for j in range(8):
                        ec = g * 8 + j
                        tr_tok = op("pe", lambda t, sl=sl, ps=ps, j=j, ec=ec:
                                    t.transpose(ps_tr[ps][:, j * 128:(j + 1) * 128],
                                                xn_t[sl][:, ec * 128:(ec + 1) * 128],
                                                id_bf[:]),
                                    deps=[t_xn, t_id] + (war if j == 0 else []),
                                    inc=(j == 7))
                    rPTR.wrote(ps, tr_tok)
                    rXN.read(sl, tr_tok)
                    dst = slab3d[:, g * 8:(g + 1) * 8, ot * 128:(ot + 1) * 128]
                    src = ps_tr[ps][:].rearrange("p (a t) -> p a t", a=8)
                    t_ev = op("dve", lambda v, dst=dst, src=src: v.tensor_copy(
                        dst, src), deps=[tr_tok])
                    rPTR.read(ps, t_ev)
                    slab_tok.append(t_ev)
            fence("gp")
            efence()
            bld.flush()

        q_sb = top.enter_context(nc.sbuf_tensor("q_sb", [128, H * 1024], BF16))
        q_tok = [None] * H

        with ExitStack() as sc1c:
            e1 = sc1c.enter_context
            wq_t = [e1(nc.sbuf_tensor(f"wq_t{i}", [128, NEC * 128], BF16)) for i in range(2)]
            ps_q = [e1(nc.psum_tensor(f"ps_q{i}", [128, 512], F32)) for i in range(2)]
            rWQ, rPQ = Ring(2), Ring(2)
            gate1c = bld.gate()
            for h in range(H):
                sl = h % 2
                src = wq[h].rearrange("(ec p) f -> p ec f", p=128)
                dst = wq_t[sl][:].rearrange("p (ec f) -> p ec f", ec=NEC)
                t_w = op("gp", lambda g, src=src, dst=dst: g.dma_start(
                    out=dst, in_=src, max_dma_last_dim=8192),
                    deps=rWQ.war(sl) + gate1c)
                rWQ.wrote(sl, t_w)
                for th in range(2):
                    mm_tok = None
                    war = rPQ.war(th)
                    for ec in range(NEC):
                        mm_tok = op("pe", lambda t, sl=sl, th=th, ec=ec: t.matmul(
                            ps_q[th][:],
                            wq_t[sl][:, ec * 128:(ec + 1) * 128],
                            xnt_slab[:, ec * 1024 + th * 512: ec * 1024 + th * 512 + 512],
                            start=(ec == 0), stop=(ec == NEC - 1)),
                            deps=[bld.dset(t_w)] + slab_tok + (war if ec == 0 else []),
                            inc=(ec == NEC - 1))
                    rPQ.wrote(th, mm_tok)
                    rWQ.read(sl, mm_tok)
                    t_qe = op("dve", lambda v, h=h, th=th: v.tensor_copy(
                        q_sb[:, h * 1024 + th * 512: h * 1024 + th * 512 + 512],
                        ps_q[th][:]), deps=[mm_tok])
                    rPQ.read(th, t_qe)
                    q_tok[h] = t_qe
            fence("gp")
            efence()
            bld.flush()

        if debug:
            g0 = bld.gate()
            op("sync", lambda s: s.dma_start(out=dbg_xnt, in_=xnt_slab[:]), deps=g0)
            op("sync", lambda s: s.dma_start(out=dbg_q, in_=q_sb[:]), deps=g0)
            op("sync", lambda s: s.dma_start(out=dbg_k, in_=k_fm[:]), deps=g0)
            op("sync", lambda s: s.dma_start(out=dbg_v, in_=v_tm[:]), deps=g0)
            bld.flush()

        attn_wr = [None] * H
        with ExitStack() as sc2:
            e2 = sc2.enter_context
            NP2 = 3
            NS2 = 4
            P_u = [e2(nc.sbuf_tensor(f"P_u{i}", [128, S], BF16)) for i in range(NP2)]
            vscr = [e2(nc.sbuf_tensor(f"vscr{i}", [128, 512], F32)) for i in range(NS2)]
            pt_sb = [e2(nc.sbuf_tensor(f"pt_sb{i}", [128, 1024], BF16)) for i in range(2)]
            sums = [e2(nc.sbuf_tensor(f"sums{i}", [128, 8], F32)) for i in range(NP2)]
            stot = [e2(nc.sbuf_tensor(f"stot{i}", [128, 1], F32)) for i in range(NP2)]
            rsum = [e2(nc.sbuf_tensor(f"rsum{i}", [128, 1], F32)) for i in range(NP2)]
            attn_t = [e2(nc.sbuf_tensor(f"attn_t{i}", [128, 128], BF16)) for i in range(NP2)]
            ps_s = [e2(nc.psum_tensor(f"ps_s{i}", [128, 512], F32)) for i in range(NS2)]
            ps_pt = [e2(nc.psum_tensor(f"ps_pt{i}", [128, 1024], BF16)) for i in range(2)]
            ps_av = [e2(nc.psum_tensor(f"ps_av{i}", [128, 128], F32)) for i in range(2)]

            rS, rV = Ring(NS2), Ring(NS2)
            rP, rSums, rStot, rRsum, rAT = (Ring(NP2) for _ in range(5))
            rPT, rPTS, rAV = (Ring(2) for _ in range(3))
            gsc = [0]

            gate2 = bld.gate()
            units = [(d, h) for d in range(1, NOWN + 1) for h in range(H)]

            def emit_A(ui):
                d, h = units[ui]
                u = ui % NP2
                exp_toks = []
                war_p = rP.war(u)
                war_sums = rSums.war(u)
                for kt in range(d):
                    m = d - 1 - kt
                    sl = gsc[0] % NS2
                    gsc[0] += 1
                    t_s = op("pe", lambda t, d=d, h=h, kt=kt, sl=sl: t.matmul(
                        ps_s[sl][:],
                        q_sb[:, h * 1024 + (d - 1) * 128: h * 1024 + d * 128],
                        k_fm[:, kt * 512:(kt + 1) * 512],
                        start=True, stop=True),
                        deps=[q_tok[h]] + [k_tok[i] for i in range(kt * 4, kt * 4 + 4)]
                             + rS.war(sl) + gate2)
                    rS.wrote(sl, t_s)
                    dmoff = 512 if m == 0 else 0
                    t_v = op("dve", lambda v, sl=sl, dmoff=dmoff, h=h: v.scalar_tensor_tensor(
                        vscr[sl][:], cst_dm[:, dmoff:dmoff + 512],
                        float(slopes[h]), ps_s[sl][:], ALU.mult, ALU.add),
                        deps=[t_s, t_dm] + rV.war(sl))
                    rV.wrote(sl, t_v)
                    rS.read(sl, t_v)
                    bcol = h * 8 + m
                    t_e = op("act", lambda s, u=u, kt=kt, sl=sl, bcol=bcol: s.activation(
                        P_u[u][:, kt * 512:(kt + 1) * 512], vscr[sl][:], AF.Exp,
                        bias=bias_sb[:, bcol:bcol + 1],
                        accum_out=sums[u][:, kt:kt + 1]),
                        deps=[t_v, t_bias] + (war_p + war_sums if kt == 0 else []))
                    rV.read(sl, t_e)
                    exp_toks.append(t_e)
                rP.wrote(u, exp_toks[-1])
                rSums.wrote(u, exp_toks[-1])
                return exp_toks

            def emit_B(ui, exp_toks):
                d, h = units[ui]
                u = ui % NP2
                u2 = ui % 2
                t_red = op("dve", lambda v, u=u, d=d: v.reduce_sum(
                    stot[u][:], sums[u][:, 0:d], mybir.AxisListType.X),
                    deps=exp_toks + rStot.war(u))
                rStot.wrote(u, t_red)
                rSums.read(u, t_red)
                t_red2 = op("dve", lambda v, u=u: v.tensor_scalar(
                    stot[u][:], stot[u][:], 1e-30, None, ALU.add), deps=[t_red])
                rStot.wrote(u, t_red2)
                t_rc = op("dve", lambda v, u=u: v.reciprocal(
                    rsum[u][:], stot[u][:]), deps=[t_red2] + rRsum.war(u))
                rRsum.wrote(u, t_rc)
                rStot.read(u, t_rc)
                t_nr = op("dve", lambda v, u=u, d=d: v.tensor_scalar(
                    P_u[u][:, 0:d * 512], P_u[u][:, 0:d * 512],
                    rsum[u][:, 0:1], None, ALU.mult), deps=[t_rc])
                rRsum.read(u, t_nr)
                rP.wrote(u, t_nr)
                nch = d * 4
                av_tok = None
                war_av = rAV.war(u2)
                for bank0 in range(0, nch, 8):
                    bp = (bank0 // 8) % 2
                    nb = min(8, nch - bank0)
                    tr_tok = None
                    war_pt = rPT.war(bp)
                    for j in range(nb):
                        ch = bank0 + j
                        tr_tok = op("pe", lambda t, bp=bp, j=j, u=u, ch=ch:
                                    t.transpose(ps_pt[bp][:, j * 128:(j + 1) * 128],
                                                P_u[u][:, ch * 128:(ch + 1) * 128],
                                                id_bf[:]),
                                    deps=[t_nr, t_id] + (war_pt if j == 0 else []),
                                    inc=(j == nb - 1))
                    rPT.wrote(bp, tr_tok)
                    rP.read(u, tr_tok)
                    t_pe = op("act", lambda s, bp=bp, nb=nb: s.copy(
                        pt_sb[bp][:, 0:nb * 128], ps_pt[bp][:, 0:nb * 128]),
                        deps=[tr_tok] + rPTS.war(bp))
                    rPTS.wrote(bp, t_pe)
                    rPT.read(bp, t_pe)
                    for j in range(nb):
                        ch = bank0 + j
                        av_tok = op("pe", lambda t, u2=u2, bp=bp, j=j, ch=ch, first=(ch == 0), last=(ch == nch - 1): t.matmul(
                            ps_av[u2][:],
                            v_tm[:, ch * 128:(ch + 1) * 128],
                            pt_sb[bp][:, j * 128:(j + 1) * 128],
                            start=first, stop=last),
                            deps=[t_pe, kv_tok[ch]] + (war_av if ch == 0 else []),
                            inc=(j == nb - 1))
                    rPTS.read(bp, av_tok)
                rAV.wrote(u2, av_tok)
                t_ae = op("dve", lambda v, u=u, u2=u2: v.tensor_copy(
                    attn_t[u][:], ps_av[u2][:]), deps=[av_tok] + rAT.war(u))
                rAT.wrote(u, t_ae)
                rAV.read(u2, t_ae)
                t_aw = op("sync", lambda s, u=u, h=h, d=d: s.dma_start(
                    out=attn_d[h, :, (d - 1) * 128:d * 128], in_=attn_t[u][:]),
                    deps=[t_ae])
                rAT.read(u, t_aw)
                attn_wr[h] = t_aw

            LAG = 2
            prevs = []
            for ui in range(len(units)):
                toks = emit_A(ui)
                prevs.append((ui, toks))
                if len(prevs) > LAG:
                    emit_B(*prevs.pop(0))
            for p in prevs:
                emit_B(*p)
            efence()
            bld.flush()

        s_wr = {}
        with ExitStack() as scw:
            e2 = scw.enter_context
            wc_t = [e2(nc.sbuf_tensor(f"wc_t{i}", [128, 512], BF16)) for i in range(4)]
            at_t = [e2(nc.sbuf_tensor(f"at_t{i}", [128, 1024], BF16)) for i in range(4)]
            s_t = [e2(nc.sbuf_tensor(f"s_t{i}", [128, 512], F32)) for i in range(2)]
            ps_wc = [e2(nc.psum_tensor(f"ps_wc{i}", [128, 512], F32)) for i in range(8)]
            rWC, rAT2, rST = Ring(4), Ring(4), Ring(2)
            rPW = Ring(8)
            gatew = bld.gate()
            for ep in range(NEP):
                stop_toks = [None] * 8
                for hc in range(H):
                    sl = hc % 4
                    t_w = op("gp", lambda g, hc=hc, ep=ep, sl=sl: g.dma_start(
                        out=wc_t[sl][:],
                        in_=wc[hc * 128:(hc + 1) * 128, ep * 512:(ep + 1) * 512]),
                        deps=rWC.war(sl) + gatew)
                    rWC.wrote(sl, t_w)
                    t_a = op("sync", lambda s, hc=hc, sl=sl: s.dma_start(
                        out=at_t[sl][:], in_=attn_d[hc]),
                        deps=[attn_wr[hc]] + rAT2.war(sl) + gatew)
                    rAT2.wrote(sl, t_a)
                    d_w = bld.dset(t_w)
                    d_a = bld.dset(t_a)
                    for dd in range(8):
                        war = rPW.war(dd) if hc == 0 else []
                        tok = op("pe", lambda t, sl=sl, dd=dd, hc=hc: t.matmul(
                            ps_wc[dd][:],
                            at_t[sl][:, dd * 128:(dd + 1) * 128],
                            wc_t[sl][:],
                            start=(hc == 0), stop=(hc == H - 1)),
                            deps=[d_w, d_a] + war,
                            inc=(hc == H - 1 or dd == 7))
                        if dd == 7:
                            rWC.read(sl, tok)
                            rAT2.read(sl, tok)
                        if hc == H - 1:
                            stop_toks[dd] = tok
                            rPW.wrote(dd, tok)
                fence("sync")
                for dd in range(8):
                    sl = dd % 2
                    t_se = op("act", lambda s, dd=dd, sl=sl: s.copy(
                        s_t[sl][:], ps_wc[dd][:]),
                        deps=[stop_toks[dd]] + rST.war(sl))
                    rST.wrote(sl, t_se)
                    t_sw = op("sync", lambda s, dd=dd, ep=ep, sl=sl: s.dma_start(
                        out=s_d[dd * 128:(dd + 1) * 128, ep * 512:(ep + 1) * 512],
                        in_=s_t[sl][:]), deps=[t_se])
                    rPW.read(dd, t_sw)
                    rST.read(sl, t_sw)
                    s_wr[(dd, ep)] = t_sw
            efence()
            bld.flush()

        h_wr = [None] * NFC
        with ExitStack() as s3a:
            e3 = s3a.enter_context
            wg_t = [e3(nc.sbuf_tensor(f"wg_t{i}", [128, NEC * 128], BF16)) for i in range(2)]
            wu_t = [e3(nc.sbuf_tensor(f"wu_t{i}", [128, NEC * 128], BF16)) for i in range(2)]
            sg_t = [e3(nc.sbuf_tensor(f"sg_t{i}", [128, 512], F32)) for i in range(2)]
            h_t = [e3(nc.sbuf_tensor(f"h_t{i}", [128, 1024], BF16)) for i in range(2)]
            ps_f = [e3(nc.psum_tensor(f"ps_f{i}", [128, 512], F32)) for i in range(8)]
            rWG, rWU, rSG, rHT = Ring(2), Ring(2), Ring(2), Ring(2)
            rPF = Ring(8)
            gate3 = bld.gate()
            for fc in range(NFC):
                sl = fc % 2
                srcg = wg[fc]
                dstg = wg_t[sl][:].rearrange("p (ec f) -> p ec f", ec=NEC)
                t_wg = op("gp", lambda g, srcg=srcg, dstg=dstg: g.dma_start(
                    out=dstg, in_=srcg, max_dma_last_dim=8192),
                    deps=rWG.war(sl) + gate3)
                rWG.wrote(sl, t_wg)
                srcu = wu[fc]
                dstu = wu_t[sl][:].rearrange("p (ec f) -> p ec f", ec=NEC)
                t_wu = op("gp", lambda g, srcu=srcu, dstu=dstu: g.dma_start(
                    out=dstu, in_=srcu, max_dma_last_dim=8192),
                    deps=rWU.war(sl) + gate3)
                rWU.wrote(sl, t_wu)
                hm_toks = []
                for th in range(2):
                    bg = sl * 4 + th * 2
                    bu = bg + 1
                    war_g = rPF.war(bg)
                    war_u = rPF.war(bu)
                    g_tok = None
                    u_tok = None
                    for ec in range(NEC):
                        g_tok = op("pe", lambda t, sl=sl, th=th, ec=ec, bg=bg: t.matmul(
                            ps_f[bg][:],
                            wg_t[sl][:, ec * 128:(ec + 1) * 128],
                            xnt_slab[:, ec * 1024 + th * 512: ec * 1024 + th * 512 + 512],
                            start=(ec == 0), stop=(ec == NEC - 1)),
                            deps=[bld.dset(t_wg)] + slab_tok + (war_g if ec == 0 else []),
                            inc=(ec == NEC - 1))
                    for ec in range(NEC):
                        u_tok = op("pe", lambda t, sl=sl, th=th, ec=ec, bu=bu: t.matmul(
                            ps_f[bu][:],
                            wu_t[sl][:, ec * 128:(ec + 1) * 128],
                            xnt_slab[:, ec * 1024 + th * 512: ec * 1024 + th * 512 + 512],
                            start=(ec == 0), stop=(ec == NEC - 1)),
                            deps=[bld.dset(t_wu)] + slab_tok + (war_u if ec == 0 else []),
                            inc=(ec == NEC - 1))
                    rPF.wrote(bg, g_tok)
                    rPF.wrote(bu, u_tok)
                    rWG.read(sl, g_tok)
                    rWU.read(sl, u_tok)
                    ssl = th
                    t_sg = op("act", lambda s, bg=bg, ssl=ssl: s.activation(
                        sg_t[ssl][:], ps_f[bg][:], AF.Silu),
                        deps=[g_tok] + rSG.war(ssl))
                    rSG.wrote(ssl, t_sg)
                    war_h = rHT.war(sl) if th == 0 else []
                    t_hm = op("dve", lambda v, sl=sl, th=th, ssl=ssl, bu=bu: v.tensor_tensor(
                        h_t[sl][:, th * 512:(th + 1) * 512], sg_t[ssl][:],
                        ps_f[bu][:], ALU.mult), deps=[t_sg, u_tok] + war_h)
                    rSG.read(ssl, t_hm)
                    hm_toks.append((t_hm, bg, bu))
                rHT.wrote(sl, hm_toks[-1][0])
                t_hw = op("sync", lambda s, fc=fc, sl=sl: s.dma_start(
                    out=h_d[fc], in_=h_t[sl][:]), deps=[t[0] for t in hm_toks])
                rHT.read(sl, t_hw)
                for _, bg2, bu2 in hm_toks:
                    rPF.read(bg2, t_hw)
                    rPF.read(bu2, t_hw)
                h_wr[fc] = t_hw
            fence("gp")
            fence("sync")
            efence()
            bld.flush()

        with ExitStack() as s3b:
            e3 = s3b.enter_context
            h2_t = [e3(nc.sbuf_tensor(f"h2_t{i}", [128, 1024], BF16)) for i in range(4)]
            wd_t = [e3(nc.sbuf_tensor(f"wd_t{i}", [128, 512], BF16)) for i in range(4)]
            s_sl = [e3(nc.sbuf_tensor(f"s_sl{i}", [128, 512], F32)) for i in range(8)]
            x_sl = [e3(nc.sbuf_tensor(f"x_sl{i}", [128, 512], F32)) for i in range(8)]
            g_t = [e3(nc.sbuf_tensor(f"g_t{i}", [128, 512], F32)) for i in range(2)]
            o_t = [e3(nc.sbuf_tensor(f"o_t{i}", [128, 512], F32)) for i in range(2)]
            o2_t = [e3(nc.sbuf_tensor(f"o2_t{i}", [128, 512], F32)) for i in range(2)]
            ps_y = [e3(nc.psum_tensor(f"ps_y{i}", [128, 512], F32)) for i in range(8)]
            rH2, rWD, rG, rO, rO2 = Ring(4), Ring(4), Ring(2), Ring(2), Ring(2)
            rSL, rXL = Ring(8), Ring(8)
            rPY = Ring(8)
            gate3b = bld.gate()
            for ep in range(NEP):
                sx_toks = []
                for tt in range(8):
                    t_sx = op("gp", lambda g, tt=tt, ep=ep: g.dma_start(
                        out=s_sl[tt][:],
                        in_=s_d[tt * 128:(tt + 1) * 128, ep * 512:(ep + 1) * 512]),
                        deps=[s_wr[(tt, ep)]] + rSL.war(tt) + gate3b)
                    rSL.wrote(tt, t_sx)
                    t_xx = op("gp", lambda g, tt=tt, ep=ep: g.dma_start(
                        out=x_sl[tt][:],
                        in_=xown[tt][:, ep * 512:(ep + 1) * 512]),
                        deps=rXL.war(tt) + gate3b)
                    rXL.wrote(tt, t_xx)
                    sx_toks.append((t_sx, t_xx))
                stop_toks = [None] * 8
                for fc in range(NFC):
                    sl = fc % 4
                    t_h = op("sync", lambda s, fc=fc, sl=sl: s.dma_start(
                        out=h2_t[sl][:], in_=h_d[fc]),
                        deps=[h_wr[fc]] + rH2.war(sl) + gate3b)
                    rH2.wrote(sl, t_h)
                    t_w = op("gp", lambda g, ep=ep, fc=fc, sl=sl: g.dma_start(
                        out=wd_t[sl][:], in_=wd[ep, fc]),
                        deps=rWD.war(sl) + gate3b)
                    rWD.wrote(sl, t_w)
                    d_h = bld.dset(t_h)
                    d_w = bld.dset(t_w)
                    for tt in range(8):
                        war = rPY.war(tt) if fc == 0 else []
                        tok = op("pe", lambda t, sl=sl, tt=tt, fc=fc: t.matmul(
                            ps_y[tt][:],
                            h2_t[sl][:, tt * 128:(tt + 1) * 128],
                            wd_t[sl][:],
                            start=(fc == 0), stop=(fc == NFC - 1)),
                            deps=[d_h, d_w] + war,
                            inc=(fc == NFC - 1 or tt == 7))
                        if tt == 7:
                            rH2.read(sl, tok)
                            rWD.read(sl, tok)
                        if fc == NFC - 1:
                            stop_toks[tt] = tok
                            rPY.wrote(tt, tok)
                fence("sync")
                fence("gp")
                for tt in range(8):
                    sl = tt % 2
                    t_g = op("act", lambda s, tt=tt, sl=sl: s.activation(
                        g_t[sl][:], ps_y[tt][:], AF.Silu),
                        deps=[stop_toks[tt]] + rG.war(sl))
                    rG.wrote(sl, t_g)
                    if debug:
                        t_dg = op("sync", lambda s, ep=ep, tt=tt, sl=sl: s.dma_start(
                            out=dbg_g[ep, tt], in_=g_t[sl][:]), deps=[t_g])
                        rG.read(sl, t_dg)
                    d_g = bld.eset(t_g) if tt < 7 else t_g
                    t_m = op("dve", lambda v, tt=tt, sl=sl: v.tensor_mul(
                        o_t[sl][:], s_sl[tt][:], g_t[sl][:]),
                        deps=[d_g, bld.dset(sx_toks[tt][0])] + rO.war(sl))
                    rO.wrote(sl, t_m)
                    rG.read(sl, bld.eset(t_m))
                    rSL.read(tt, t_m)
                    if debug:
                        t_do = op("sync", lambda s, ep=ep, tt=tt, sl=sl: s.dma_start(
                            out=dbg_o[ep, tt], in_=o_t[sl][:]), deps=[t_m])
                        rO.read(sl, t_do)
                    t_a = op("dve", lambda v, tt=tt, sl=sl: v.tensor_add(
                        o2_t[sl][:], o_t[sl][:], x_sl[tt][:]),
                        deps=[t_m, bld.dset(sx_toks[tt][1])] + rO2.war(sl))
                    rO2.wrote(sl, t_a)
                    rO.read(sl, t_a)
                    rXL.read(tt, t_a)
                    t_ow = op("sync", lambda s, tt=tt, ep=ep, sl=sl: s.dma_start(
                        out=out[tt][:, ep * 512:(ep + 1) * 512], in_=o2_t[sl][:]),
                        deps=[t_a])
                    rO2.read(sl, t_ow)
                    rPY.read(tt, t_ow)
            final = bld.count["sync"]
            fsem = bld.sems["sync"]
            bld.raw("sync", lambda s: s.wait_ge(fsem, final))
            bld.flush()

    return nc


def _prep_shared(w_qkv, w_c, W_gate, W_up, W_down):
    wkv = np.ascontiguousarray(w_qkv[:, E:E + 256])
    wqp = np.ascontiguousarray(
        w_qkv[:, :E].reshape(E, H, 128).transpose(1, 0, 2))
    wgp = np.ascontiguousarray(
        W_gate.reshape(NFC, 128, NEC, 128).transpose(0, 3, 2, 1))
    wup = np.ascontiguousarray(
        W_up.reshape(NFC, 128, NEC, 128).transpose(0, 3, 2, 1))
    wdp = np.ascontiguousarray(
        W_down.reshape(NEP, 512, NFC, 128).transpose(0, 2, 3, 1))
    return wkv, wqp, np.ascontiguousarray(w_c), wgp, wup, wdp


def _consts_for_slot(slot):
    slopes = _alibi_slopes()
    j = np.arange(512, dtype=np.float64)[None, :]
    i = np.arange(128, dtype=np.float64)[:, None]
    gen = (j - i).astype(np.float32)
    diag = gen.copy()
    diag[(j - i) > 128 * slot] = MASK_NEG
    dm = np.concatenate([gen, diag], axis=1).astype(np.float32)
    bias = np.zeros((128, H * 8), dtype=np.float32)
    for h in range(H):
        for m in range(8):
            delta = -512.0 * m - 128.0 * slot
            bias[:, h * 8 + m] = np.float32(-BSTAT + float(slopes[h]) * delta)
    ident = np.eye(128, dtype=np.float32)
    msc = np.zeros((128, 8), dtype=np.float32)
    msc[:, 0] = 1e-5
    return dm, bias, ident, msc


def kernel(L0_attn_hidden_states, L0_attn_w_qkv, L0_attn_w_c,
           L0_ffn_W_gate, L0_ffn_W_up, L0_ffn_W_down, _debug=False):
    x = np.asarray(L0_attn_hidden_states, dtype=np.float32)
    wkv, wqp, wcc, wgp, wup, wdp = _prep_shared(
        np.asarray(L0_attn_w_qkv, dtype=np.float32),
        np.asarray(L0_attn_w_c, dtype=np.float32),
        np.asarray(L0_ffn_W_gate, dtype=np.float32),
        np.asarray(L0_ffn_W_up, dtype=np.float32),
        np.asarray(L0_ffn_W_down, dtype=np.float32))

    nc = build_nc(debug=_debug)
    in_maps = []
    for core in range(8):
        b, slot = core // 4, core % 4
        dm, bias, ident, msc = _consts_for_slot(slot)
        xb = np.ascontiguousarray(x[b])
        xown = np.ascontiguousarray(
            xb.reshape(8, 4, 128, E)[:, slot])
        in_maps.append({
            "xkv": xb, "xown": xown, "wkv": wkv, "wq": wqp, "wc": wcc,
            "wg": wgp, "wu": wup, "wd": wdp,
            "dmc": dm, "biasc": bias, "identc": ident, "mscc": msc,
        })
    res = run_bass_kernel_spmd(nc, in_maps, core_ids=list(range(8)))
    full = np.empty((B, S, E), dtype=np.float32)
    for core in range(8):
        b, slot = core // 4, core % 4
        o = np.asarray(res.results[core]["out"])
        full[b].reshape(8, 4, 128, E)[:, slot] = o
    if _debug:
        return full, res
    return full
